# version 05
# baseline (speedup 1.0000x reference)
import numpy as np
from contextlib import ExitStack

import concourse.bass as bass
import concourse.mybir as mybir
from concourse.bass_utils import run_bass_kernel_spmd

F32 = mybir.dt.float32
F32R = mybir.dt.float32r
BF16 = mybir.dt.bfloat16
I32 = mybir.dt.int32
AF = mybir.ActivationFunctionType
ALU = mybir.AluOpType
AX = mybir.AxisListType

NCORES = 8
S, D, FF, E, R, H, HD = 8192, 2048, 5632, 8, 16, 32, 64
TPC = S // NCORES
DC = D // 128
FC = FF // 128
NTT = TPC // 512
NTB = TPC // 128
WIN = 384
NKC = 7
VP = HD + 1
HVP = H * VP
MSLOPE = 2.0 ** (-0.25)
CSHIFT = -16.0
EPS = 1e-5

ENGINES = ("sp", "pe", "dve", "act", "pool")
BLOCK_OF = {"sp": "sync", "pe": "tensor", "dve": "vector", "act": "scalar",
            "pool": "gpsimd"}
DMA_SEM = {"sp": "dS", "pool": "dP", "act": "dA"}
ESZ = {F32: 4, F32R: 4, BF16: 2, I32: 4}


class Buf:
    __slots__ = ("h", "name")

    def __init__(self, h, name):
        self.h = h
        self.name = name

    def __getitem__(self, idx):
        return self.h[idx]


SERIAL = [False]
DMA_FIFO = [True]
DMA_W = {"sp": 1, "pool": 1, "act": 1}
PE_SYNC = [False]
SERIAL_PHASES = set()
SERIAL_TAGS = set()


class Prog:
    def __init__(self, nc):
        self.nc = nc
        self.ops = []
        self.last_w = {}
        self.readers = {}
        self.prev = None

    phase = "init"

    def op(self, engine, fn, reads=(), writes=(), dma=False, cc=False, tag=""):
        oid = len(self.ops)
        deps = set()
        for b in reads:
            deps.update(self.last_w.get(b.name, {}).values())
        for b in writes:
            deps.update(self.last_w.get(b.name, {}).values())
            deps.update(self.readers.get(b.name, {}).values())
        if (SERIAL[0] or self.phase in SERIAL_PHASES or tag in SERIAL_TAGS) and self.prev is not None:
            deps.add(self.prev)
        self.ops.append(dict(id=oid, engine=engine, fn=fn, dma=dma, cc=cc,
                             deps=deps, tag=tag))
        self.prev = oid
        key = engine + ("D" if dma else "")
        for b in reads:
            self.readers.setdefault(b.name, {})[key] = oid
        for b in writes:
            self.last_w[b.name] = {key: oid}
            self.readers[b.name] = {}
        return oid

    def barrier(self, eng_fn, reads, writes, tag="barrier"):
        self.op("dve", eng_fn, reads=reads, writes=writes, tag=tag)

    def finalize(self, block, sems, dbg=False):
        ops = self.ops
        dma_cum = {"sp": 0, "pool": 0, "act": 0}
        cc_cum = 0
        for o in ops:
            if o["dma"]:
                dma_cum[o["engine"]] += 16
                o["sem"], o["val"], o["inc"] = (DMA_SEM[o["engine"]],
                                                dma_cum[o["engine"]], 16)
            elif o["cc"]:
                cc_cum += 1
                o["sem"], o["val"], o["inc"] = "cc", cc_cum, 1
        needed = set()
        for o in ops:
            for pid in o["deps"]:
                po = ops[pid]
                if po["dma"] or po["cc"]:
                    continue
                if (not SERIAL[0] and not PE_SYNC[0]
                        and po["engine"] == "pe" and o["engine"] == "pe"):
                    continue
                needed.add(pid)
        rank = {e: 0 for e in ENGINES}
        for o in ops:
            if o["dma"] or o["cc"]:
                continue
            if o["id"] in needed:
                rank[o["engine"]] += 1
                o["sem"], o["val"], o["inc"] = o["engine"], rank[o["engine"]], 1
            else:
                o["sem"] = None
        hw = {}
        n_waits = 0
        dma_hist = {"sp": [], "pool": [], "act": []}
        for o in ops:
            w = {}
            if DMA_FIFO[0] and o["dma"]:
                hist = dma_hist[o["engine"]]
                wd = DMA_W[o["engine"]]
                if len(hist) >= wd:
                    w[DMA_SEM[o["engine"]]] = hist[-wd]
                hist.append(o["val"])
            for pid in o["deps"]:
                po = ops[pid]
                if po.get("sem") is None:
                    continue
                if (not SERIAL[0] and not PE_SYNC[0]
                        and not po["dma"] and not po["cc"]
                        and po["engine"] == "pe" and o["engine"] == "pe"):
                    continue
                w[po["sem"]] = max(w.get(po["sem"], 0), po["val"])
            real = {}
            for sname, v in sorted(w.items()):
                if hw.get((o["engine"], sname), 0) >= v:
                    continue
                real[sname] = v
                hw[(o["engine"], sname)] = v
            o["waits"] = real
            n_waits += len(real)
        if dbg:
            cnt = {}
            for o in ops:
                cnt[o["engine"]] = cnt.get(o["engine"], 0) + 1
            print(f"[prog] ops={len(ops)} waits={n_waits} per-engine={cnt} "
                  f"comp-ranks={rank} dma16={dma_cum} cc={cc_cum}", flush=True)

        def mk(engine):
            def body(eng):
                for o in ops:
                    if o["engine"] != engine:
                        continue
                    for sname, v in o["waits"].items():
                        eng.wait_ge(sems[sname], v)
                    try:
                        inst = o["fn"](eng)
                    except Exception as ex:
                        raise RuntimeError(
                            f"emit failed op={o['id']} tag={o['tag']}: {ex}")
                    if o.get("sem") is not None:
                        inst.then_inc(sems[o["sem"]], o["inc"])
                if engine == "sp":
                    for q, tot in dma_cum.items():
                        if tot:
                            eng.wait_ge(sems[DMA_SEM[q]], tot)
                    for e2 in ("pe", "dve", "act", "pool"):
                        if rank[e2]:
                            eng.wait_ge(sems[e2], rank[e2])
            return body

        for e in ENGINES:
            getattr(block, BLOCK_OF[e])(mk(e))


def _prod(sh):
    r = 1
    for s in sh:
        r *= s
    return r



def build_program(nc, taps=()):
    taps = set(taps)
    ctx = ExitStack()

    def dram(name, shape, dtype, kind="Internal"):
        return Buf(nc.dram_tensor(name, list(shape), dtype, kind=kind).ap(),
                   name)

    def sbuf(name, shape, dtype):
        return Buf(ctx.enter_context(nc.sbuf_tensor(name, list(shape), dtype)),
                   name)

    def psum(name, shape, dtype=F32):
        return Buf(ctx.enter_context(nc.psum_tensor(name, list(shape), dtype)),
                   name)

    xin = dram("xT", (D, TPC), F32R, "ExternalInput")
    Wat = dram("Wat", (D, 3 * D), F32, "ExternalInput")
    bat = dram("bat", (48, 128), F32, "ExternalInput")
    batT = dram("batT", (128, 48), F32, "ExternalInput")
    bprT = dram("bprT", (128, 16), F32, "ExternalInput")
    Wpr = dram("Wpr", (D, D), F32, "ExternalInput")
    bpr = dram("bpr", (16, 128), F32, "ExternalInput")
    gw_d = dram("gw", (E, D), F32, "ExternalInput")
    w1_d = dram("w1", (FF, D), F32, "ExternalInput")
    w3_d = dram("w3", (FF, D), F32, "ExternalInput")
    w2_d = dram("w2", (D, FF), F32, "ExternalInput")
    a1_d = dram("a1", (E, R, D), F32, "ExternalInput")
    b1_d = dram("b1", (E, FF, R), F32, "ExternalInput")
    a3_d = dram("a3", (E, R, D), F32, "ExternalInput")
    b3_d = dram("b3", (E, FF, R), F32, "ExternalInput")
    a2_d = dram("a2", (E, R, FF), F32, "ExternalInput")
    b2_d = dram("b2", (E, D, R), F32, "ExternalInput")
    eye_d = dram("eyec", (128, 128), F32, "ExternalInput")
    eyer_d = dram("eyerc", (128, 128), F32R, "ExternalInput")
    eye2_d = dram("eye2c", (128, 128), F32, "ExternalInput")
    mask_d = dram("maskc", (4, 128, 512), F32, "ExternalInput")
    bik_d = dram("biaskc", (128, NTT * NKC), F32, "ExternalInput")
    aq_d = dram("aqc", (1, NTT, 512), F32, "ExternalInput")
    inda_d = dram("indac", (1, 128), F32, "ExternalInput")
    indb_d = dram("indbc", (1, 128), F32, "ExternalInput")
    sel_d = dram("sel8c", (8, 8 * 128), F32, "ExternalInput")
    gix_d = dram("gidxc", (128, 1), I32, "ExternalInput")

    xout = dram("xo", (D, TPC), F32R, "ExternalOutput")

    x1d = dram("x1d", (D, TPC), F32R)
    x2d = dram("x2d", (D, TPC), F32R)
    w1T = dram("w1T", (FC, DC, 128, 128), BF16)
    w3T = dram("w3T", (FC, DC, 128, 128), BF16)
    w2T = dram("w2T", (FC, DC, 128, 128), BF16)
    b1T = dram("b1T", (16, E, FF), BF16)
    b3T = dram("b3T", (16, E, FF), BF16)
    a2Td = dram("a2Td", (E, 128, FC * R), BF16)
    b2Td = dram("b2Td", (16, E, D), BF16)
    gwTd = dram("gwTd", (128, DC * E), F32)
    a1Td = dram("a1Td", (3, 128, DC * 96), BF16)
    a3Td = dram("a3Td", (3, 128, DC * 96), BF16)
    qfd = dram("qfd", (D, TPC), BF16)
    kfd = dram("kfd", (D, TPC), BF16)
    vtd = dram("vtd", (TPC, HVP), BF16)
    kbin = dram("kbin", (D, WIN), BF16)
    vbin = dram("vbin", (WIN, HVP), BF16)
    kbg = dram("kbg", (8 * 16, D * WIN // 16), BF16)
    vbg = dram("vbg", (8 * 16, WIN * HVP // 16), BF16)
    kbp = dram("kbp", (D, WIN), BF16)
    vbp = dram("vbp", (WIN, HVP), BF16)

    eye = sbuf("eye", (128, 128), F32)
    eyer = sbuf("eyer", (128, 128), F32R)
    eye16 = sbuf("eye16", (128, 128), BF16)
    eye16x2 = sbuf("eye16x2", (128, 128), BF16)
    ones1 = sbuf("ones1", (1, 128), F32)
    ones1h = sbuf("ones1h", (1, 128), BF16)
    onesc = sbuf("onesc", (128, 1), F32)
    masks = sbuf("masks", (128, 4, 512), BF16)
    biask = sbuf("biask", (128, NTT * NKC), F32)
    aq = sbuf("aq", (1, NTT, 512), BF16)
    indA = sbuf("indA", (1, 128), F32)
    indB = sbuf("indB", (1, 128), F32)
    gidx = sbuf("gidx", (128, 1), I32)
    bav = sbuf("bav", (128, 48), F32)
    bavq = sbuf("bavq", (128, 16), F32)
    vbias = sbuf("vbias", (1, 4, 512), BF16)
    bpv = sbuf("bpv", (128, 16), F32)
    xch = sbuf("xch", (128, 2, 512), F32R)
    base1s = sbuf("base1s", (128, 512), F32R)
    base3s = sbuf("base3s", (128, 512), F32R)
    ost = sbuf("ost", (128, 2, 512), F32R)
    sqs = sbuf("sqs", (128, 1, 512), F32R)
    t1s = sbuf("t1s", (128, 2, 512), F32)
    xn16 = sbuf("xn16", (128, DC, 512), BF16)
    rowA = sbuf("rowA", (1, 512), F32)
    rowB = sbuf("rowB", (1, 512), F32)
    meanB = sbuf("meanB", (128, 512), F32)
    rstdB = sbuf("rstdB", (128, 512), F32)
    qkst = sbuf("qkst", (128, 2, 512), BF16)
    epsb = sbuf("epsb", (1, 1), F32)
    m1t = sbuf("m1t", (128, 1), F32)
    m1n = sbuf("m1n", (128, 1), F32)
    m2t = sbuf("m2t", (128, 1), F32)
    cntt = sbuf("cntt", (128, 1), F32)
    dif1 = sbuf("dif1", (128, 1), F32)
    expt = sbuf("expt", (128, E), F32)
    exp2t = sbuf("exp2t", (128, 1), F32)
    dinv = sbuf("dinv", (128, 1), F32)
    mskt = sbuf("mskt", (128, E), F32)
    wtmp = sbuf("wtmp", (128, E), F32)
    ltok = sbuf("ltok", (128, 4, E), F32)
    wetok = sbuf("wetok", (128, 4, E), F32)

    AR_BYTES = 149 * 1024
    arena = sbuf("AR", (128, AR_BYTES // 2), BF16)
    ar32 = arena.h[:, :].bitcast(F32)
    ar16 = arena.h[:, :]

    def carve(name, byte_off, parts, shape, dtype):
        n = _prod(shape)
        esz = ESZ[dtype]
        assert byte_off % 64 == 0
        assert byte_off + n * esz <= AR_BYTES, (name, byte_off, n * esz)
        base = ar16 if dtype == BF16 else arena.h[:, :].bitcast(dtype)
        v = base[0:parts, byte_off // esz: byte_off // esz + n]
        if len(shape) > 1:
            spec = "p (" + " ".join(f"a{i}" for i in range(len(shape))) + ") -> p " + \
                   " ".join(f"a{i}" for i in range(len(shape)))
            kw = {f"a{i}": shape[i] for i in range(len(shape))}
            v = v.rearrange(spec, **kw)
        return Buf(v, name)

    def align(x):
        return (x + 63) & ~63

    off = 0

    def take(name, parts, shape, dtype):
        nonlocal off
        b = carve(name, off, parts, shape, dtype)
        off = align(off + _prod(shape) * ESZ[dtype])
        return b

    off = 0
    zsum = take("zsum", 128, (FC, 512), BF16)
    weB = take("weB", 128, (E, 512), F32)
    vpw = take("vpw", 16, (E, 512), BF16)
    accf = take("accf", 128, (512,), F32)
    ssil = take("ssil", 128, (2, 512), BF16)
    swt = take("swt", 128, (2, 512), BF16)
    zt = take("zt", 128, (2, 512), BF16)
    a2Ts = take("a2Ts", 128, (E, FC * R), BF16)
    lgr = take("lgr", 8, (512,), F32)
    weT = take("weT", 8, (512,), F32)
    u1g = [take(f"u1g{g}", 96, (512,), BF16) for g in range(3)]
    u3g = [take(f"u3g{g}", 96, (512,), BF16) for g in range(3)]
    a1T = [take(f"a1T{g}", 128, (DC, 96), BF16) for g in range(3)]
    a3T = [take(f"a3T{g}", 128, (DC, 96), BF16) for g in range(3)]
    gwT = take("gwT", 128, (DC, E), F32)
    gwhi = take("gwhi", 128, (DC, E), BF16)
    gwlo = take("gwlo", 128, (DC, E), BF16)
    t1hi = take("t1hi", 128, (2, 512), BF16)
    t1lo = take("t1lo", 128, (2, 512), BF16)
    sel8 = take("sel8", 8, (8 * 128,), F32)
    ffn_stream_off = off
    off = ffn_stream_off
    w1Ts = take("w1Ts", 128, (2, DC, 128), BF16)
    w3Ts = take("w3Ts", 128, (2, DC, 128), BF16)
    b1Ts = take("b1Ts", 96, (2, E, 128), BF16)
    b3Ts = take("b3Ts", 96, (2, E, 128), BF16)
    ffn_a_end = off
    off = ffn_stream_off
    w2Ts = take("w2Ts", 128, (2, 8, 128), BF16)
    b2Ts = take("b2Ts", 16, (E, 512), BF16)
    ffn_b_end = off
    ffn_end = max(ffn_a_end, ffn_b_end)
    assert ffn_end <= AR_BYTES, ffn_end

    off = 0
    wnat = take("wnat", 128, (2, D), BF16)
    wtt = take("wtt", 128, (2, DC, 128), BF16)
    bnat = take("bnat", 128, (FC, R), BF16)
    a2nat = take("a2nat", 16, (FF,), BF16)
    b2nat = take("b2nat", 128, (DC, R), BF16)
    tev = take("tev", 16, (FC * 128,), BF16)
    tev2 = take("tev2", 128, (DC * 96,), BF16)
    tev3 = take("tev3", 16, (D,), BF16)
    apad = take("apad", 96, (D,), F32)
    gws = take("gws", 8, (D,), F32)
    gwstg = take("gwstg", 128, (DC * E,), F32)
    cache_end = off
    assert cache_end <= AR_BYTES

    off = 0
    qw = take("qw", 128, (16, 512), BF16)
    kw = take("kw", 128, (16, NKC * 128), BF16)
    vw = take("vw", 128, (NKC, HVP), BF16)
    was = take("was", 128, (2, DC, 128), BF16)
    wvs = take("wvs", 128, (DC, 512), BF16)
    vstage = take("vstage", 128, (HVP,), BF16)
    rsiA = take("rsiA", 1, (512,), F32)
    rsiB = take("rsiB", 1, (512,), F32)
    core_off = off
    pP = take("pP", 128, (4, 512), BF16)
    rsb = take("rsb", 128, (512,), F32)
    att = take("att", 128, (16, 512), BF16)
    attn_end = off
    off = core_off
    gstage = take("gstage", 128, (6144,), BF16)
    gstage2 = take("gstage2", 128, (6240,), BF16)
    attn_end = max(attn_end, off)
    assert attn_end <= AR_BYTES, attn_end

    CACHE_BUFS = [wnat, wtt, bnat, a2nat, b2nat, tev, tev2, tev3, apad, gws]
    FFN_CORE = ([zsum, weB, vpw, accf, ssil, swt, zt, a2Ts,
                 lgr, weT, gwT, sel8] + u1g + u3g + a1T + a3T)
    FFN_A_BUFS = [w1Ts, w3Ts, b1Ts, b3Ts]
    FFN_B_BUFS = [w2Ts, b2Ts]
    ATTN_BUFS = [qw, kw, vw, was, wvs, vstage, pP, rsb, att, gstage,
                 gstage2, rsiA, rsiB]

    bank = [psum(f"bk{i}", (128, 512)) for i in range(8)]

    def bkT(p0, p1, f0, f1):
        return bank[7].h.bitcast(BF16)[p0:p1, f0:f1]

    bkTbuf = bank[7]

    sems = {k: ctx.enter_context(nc.semaphore(k))
            for k in ["pe", "dve", "act", "pool", "dS", "dP", "dA", "cc"]}
    block = ctx.enter_context(nc.Block())
    p = Prog(nc)

    def phase_barrier(olds, news, tag):
        p.op("dve", lambda e: e.memset(arena.h[0:1, 0:1], 0.0),
             reads=olds, writes=list(olds) + list(news) + [arena], tag=tag)

    def dma(q, out, in_, reads=(), writes=(), tag=""):
        p.op(q, lambda e, o=out, i=in_: e.dma_start(out=o, in_=i),
             reads=reads, writes=writes, dma=True, tag=tag)

    def dmat(out, in_, reads=(), writes=(), tag=""):
        p.op("act", lambda e, o=out, i=in_: e.dma_start_transpose(o, i),
             reads=reads, writes=writes, dma=True, tag=tag)

    def mm(out, lhsT, rhs, start, stop, reads=(), writes=(), tag=""):
        p.op("pe", lambda e, o=out, l=lhsT, r=rhs, s1=start, s2=stop:
             e.matmul(o, l, r, start=s1, stop=s2),
             reads=reads, writes=writes, tag=tag)

    def tr(out, in_, ident, start, reads=(), writes=(), tag=""):
        if start:
            p.op("pe", lambda e, o=out, i=in_, d=ident: e.transpose(o, i, d),
                 reads=reads, writes=writes, tag=tag)
        else:
            p.op("pe", lambda e, o=out, i=in_, d=ident:
                 e.matmul(o, i, d, is_transpose=True, start=False, stop=True),
                 reads=reads, writes=writes, tag=tag)

    def cp(eng, out, in_, reads, writes, tag=""):
        p.op(eng, lambda e, o=out, i=in_: e.tensor_copy(out=o, in_=i),
             reads=reads, writes=writes, tag=tag)

    def tt(eng, out, in0, in1, op, reads, writes, tag=""):
        p.op(eng, lambda e, o=out, a=in0, b=in1, c=op:
             e.tensor_tensor(out=o, in0=a, in1=b, op=c),
             reads=reads, writes=writes, tag=tag)

    def ts1(eng, out, in0, s1, op0, reads, writes, tag=""):
        p.op(eng, lambda e, o=out, a=in0, x=s1, c0=op0:
             e.tensor_scalar(out=o, in0=a, scalar1=x, scalar2=None, op0=c0),
             reads=reads, writes=writes, tag=tag)

    def stt(eng, out, in0, sc, in1, op0, op1, reads, writes, tag=""):
        p.op(eng, lambda e, o=out, a=in0, s=sc, b=in1, c0=op0, c1=op1:
             e.scalar_tensor_tensor(out=o, in0=a, scalar=s, in1=b,
                                    op0=c0, op1=c1),
             reads=reads, writes=writes, tag=tag)

    def act(out, in_, func, bias=0.0, scale=1.0, reads=(), writes=(), tag=""):
        p.op("act", lambda e, o=out, i=in_, f=func, b=bias, s=scale:
             e.activation(out=o, in_=i, func=f, bias=b, scale=s),
             reads=reads, writes=writes, tag=tag)

    def recip(out, in_, reads, writes, tag=""):
        p.op("dve", lambda e, o=out, i=in_: e.reciprocal(out=o, in_=i),
             reads=reads, writes=writes, tag=tag)

    def rmax(out, in_, reads, writes, tag=""):
        p.op("dve", lambda e, o=out, i=in_: e.reduce_max(o, i, axis=AX.X),
             reads=reads, writes=writes, tag=tag)

    def memset(buf, val, apx=None):
        a = apx if apx is not None else buf[:]
        p.op("dve", lambda e, x=a, v=val: e.memset(x, v), writes=[buf])

    def r32(apx):
        return apx.bitcast(F32R)

    tapd = {}
    for name, shape, dt_ in [("ln1", (D, TPC), BF16), ("we", (TPC, E), F32),
                             ("u1", (3 * 96, 512), BF16),
                             ("zs0", (FF, 512), BF16),
                             ("vp0", (16 * E, 512), BF16),
                             ("x1", (D, TPC), F32R), ("qf", (D, TPC), BF16),
                             ("kf", (D, TPC), BF16), ("vt", (TPC, HVP), BF16),
                             ("at0", (D, 512), BF16), ("x2", (D, TPC), F32R),
                             ("gwt", (128, DC * E), F32),
                             ("a1t", (128, DC * 96), BF16),
                             ("lgr0", (8, 512), F32),
                             ("ltk0", (128, 4 * E), F32),
                             ("m1x", (128, 1), F32), ("m2x", (128, 1), F32),
                             ("dvx", (128, 1), F32), ("gws0", (8, 2048), F32),
                             ("gstg", (128, 128), F32),
                             ("gwtd", (128, 128), F32),
                             ("a2t0", (128, FC * R), BF16),
                             ("b2t0", (16, D), BF16),
                             ("b1t0", (16, FF), BF16)]:
        if name in taps:
            tapd[name] = dram("tp_" + name, shape, dt_, "ExternalOutput")

    dma("sp", eye[:], eye_d.h[:, :], writes=[eye])
    dma("sp", eyer[:], eyer_d.h[:, :], writes=[eyer])
    dma("pool", eye16[:], eye_d.h[:, :], writes=[eye16])
    dma("pool", eye16x2[:], eye2_d.h[:, :], writes=[eye16x2])
    dma("pool", masks[:], mask_d.h.rearrange("m p n -> p m n"), writes=[masks])
    dma("sp", biask[:], bik_d.h[:, :], writes=[biask])
    dma("pool", aq[:], aq_d.h[:, :, :], writes=[aq])
    dma("sp", indA[:], inda_d.h[:, :], writes=[indA])
    dma("sp", indB[:], indb_d.h[:, :], writes=[indB])
    dma("sp", gidx[:], gix_d.h[:, :], writes=[gidx])
    dma("sp", bav[:], batT.h[:, :], writes=[bav])
    dma("pool", vbias[:],
        bat.h.rearrange("(g r n) p -> g r (n p)", g=3, n=4)[2:3, :, :],
        writes=[vbias])
    dma("sp", bpv[:], bprT.h[:, :], writes=[bpv])
    memset(epsb, EPS)
    ts1("dve", bavq[:], bav[:, 0:16], 0.125, ALU.mult,
        reads=[bav], writes=[bavq])
    memset(ones1, 1.0)
    memset(ones1h, 1.0)
    memset(onesc, 1.0)

    dma("sp", gws[:], gw_d.h[:, :], writes=[gws])
    for c4 in range(4):
        for j in range(4):
            dc = 4 * c4 + j
            tr(bank[0][:, 8 * j:8 * j + 8], gws[0:E, 128 * dc:128 * dc + 128],
               eye[0:8, 0:8], start=(j == 0), reads=[gws, eye], writes=[bank[0]])
        cp("dve", gwstg[:, 32 * c4:32 * c4 + 32], bank[0][:, 0:32],
           reads=[bank[0]], writes=[gwstg])

    if "gws0" in tapd:
        dma("sp", tapd["gws0"].h[:, :], gws[:, :], reads=[gws])
    if "gstg" in tapd:
        dma("sp", tapd["gstg"].h[:, :], gwstg[:, :], reads=[gwstg])
    dma("sp", gwTd.h[:, :], gwstg[:, :], reads=[gwstg], writes=[gwTd], tag="gw_st")
    if "gwtd" in tapd:
        dma("sp", tapd["gwtd"].h[:, :], gwTd.h[:, :], reads=[gwstg])
    for wsrc, wdst, colsrc in ((w1_d, w1T, False), (w3_d, w3T, False),
                               (w2_d, w2T, True)):
        for fc in range(FC):
            sl = fc % 2
            if colsrc:
                src = wsrc.h.rearrange("(c p) f -> p c f", p=128)[
                    :, :, 128 * fc:128 * fc + 128]
                dst = wnat.h[:, sl, :].rearrange("p (c f) -> p c f", f=128)
            else:
                src = wsrc.h[128 * fc:128 * fc + 128, :]
                dst = wnat.h[:, sl, :]
            dma("pool", dst, src, reads=[wnat], writes=[wnat], tag="wc_ld")
            for half in range(2):
                for j in range(8):
                    dc = 8 * half + j
                    tr(bkT(0, 128, 128 * j, 128 * j + 128),
                       wnat[:, sl, 128 * dc:128 * dc + 128], eye16[:],
                       start=(j == 0), reads=[wnat, eye16], writes=[bkTbuf],
                       tag="wc_tr")
                cp("dve", wtt[:, sl, 8 * half:8 * half + 8, :],
                   bkT(0, 128, 0, 1024), reads=[bkTbuf], writes=[wtt],
                   tag="wc_ev")
            dma("sp", wdst.h[fc].rearrange("dc p f -> p dc f"), wtt[:, sl, :, :],
                reads=[wtt], writes=[wdst], tag="wc_st")

    EPERM = [0, 3, 6, 1, 4, 7, 2, 5]
    for bsrc, bdst in ((b1_d, b1T), (b3_d, b3T)):
        for pos in range(E):
            e_ = EPERM[pos]
            dma("pool", bnat[:], bsrc.h[e_].rearrange("(c p) r -> p c r", p=128),
                reads=[bnat], writes=[bnat], tag="bT_ld")
            for q4 in range(11):
                for j in range(4):
                    fc = 4 * q4 + j
                    tr(bkT(0, 16, 128 * j, 128 * j + 128), bnat[:, fc, :],
                       eye16[:], start=(j == 0),
                       reads=[bnat, eye16], writes=[bkTbuf])
                cp("dve", tev[0:16, 512 * q4:512 * q4 + 512], bkT(0, 16, 0, 512),
                   reads=[bkTbuf], writes=[tev])
            dma("sp", bdst.h[:, pos, :], tev[0:16, :], reads=[tev], writes=[bdst], tag="bT_st")

    for e_ in range(E):
        dma("pool", a2nat[:], a2_d.h[e_, :, :], reads=[a2nat], writes=[a2nat],
            tag="a2_ld")
        for q8 in range(6):
            nj = 8 if q8 < 5 else 4
            for j in range(nj):
                fc = 8 * q8 + j
                tr(bkT(0, 128, R * j, R * j + R),
                   a2nat[0:16, 128 * fc:128 * fc + 128], eye16[0:16, 0:16],
                   start=(j == 0), reads=[a2nat, eye16], writes=[bkTbuf])
            cp("dve", tev2[:, R * 8 * q8:R * (8 * q8 + nj)],
               bkT(0, 128, 0, R * nj), reads=[bkTbuf], writes=[tev2])
        dma("sp", a2Td.h[e_, :, :], tev2[:, 0:FC * R], reads=[tev2], writes=[a2Td], tag="a2_st")

    for e_ in range(E):
        dma("pool", b2nat[:], b2_d.h[e_].rearrange("(c p) r -> p c r", p=128),
            reads=[b2nat], writes=[b2nat], tag="b2_ld")
        for q4 in range(4):
            for j in range(4):
                dc = 4 * q4 + j
                tr(bkT(0, 16, 128 * j, 128 * j + 128), b2nat[:, dc, :],
                   eye16[:], start=(j == 0), reads=[b2nat, eye16],
                   writes=[bkTbuf])
            cp("dve", tev3[0:16, 512 * q4:512 * q4 + 512], bkT(0, 16, 0, 512),
               reads=[bkTbuf], writes=[tev3])
        dma("sp", b2Td.h[:, e_, :], tev3[0:16, :], reads=[tev3], writes=[b2Td], tag="b2_st")

    if "a2t0" in tapd:
        dma("sp", tapd["a2t0"].h[:, :], a2Td.h[0, :, :], reads=[tev2])
    if "b2t0" in tapd:
        dma("sp", tapd["b2t0"].h[:, :], b2Td.h[:, 0, :], reads=[tev3])
    if "b1t0" in tapd:
        dma("sp", tapd["b1t0"].h[:, :], b1T.h[:, 0, :], reads=[tev])
    for asrc, adst in ((a1_d, a1Td), (a3_d, a3Td)):
        for g in range(3):
            gsz = 3 if g < 2 else 2
            memset(apad, 0.0)
            for gi in range(gsz):
                dma("sp", apad.h[32 * gi:32 * gi + 16, :],
                    asrc.h[3 * g + gi, :, :], reads=[apad], writes=[apad],
                    tag="a_ld")
            for q4 in range(4):
                for j in range(4):
                    dc = 4 * q4 + j
                    tr(bkT(0, 96, 128 * j, 128 * j + 96)
                       if False else bank[0][:, 96 * j:96 * j + 96],
                       apad[0:96, 128 * dc:128 * dc + 128], eye[0:96, 0:96],
                       start=(j == 0), reads=[apad, eye], writes=[bank[0]])
                cp("dve", tev2[:, 96 * 4 * q4:96 * (4 * q4 + 4)],
                   bank[0][:, 0:384], reads=[bank[0]], writes=[tev2])
            dma("sp", adst.h[g, :, :], tev2[:, 0:DC * 96], reads=[tev2],
                writes=[adst], tag="aT_st")

    def layernorm(xsrc, tt_i, with_gate):
        p.phase = "ln"
        c0 = 512 * tt_i
        for dc in range(DC):
            sl = dc % 2
            dma("pool", xch[:, sl, :], xsrc.h[128 * dc:128 * dc + 128, c0:c0 + 512],
                reads=[xch, xsrc], writes=[xch], tag="ln_ldx")
            mm(bank[0][0:1, :], r32(onesc[:]), xch[:, sl, :],
               start=(dc == 0), stop=(dc == DC - 1),
               reads=[onesc, xch], writes=[bank[0]], tag="ln_sum")
            act(sqs[:, 0, :], xch[:, sl, :], AF.Square,
                reads=[xch], writes=[sqs], tag="ln_sq")
            mm(bank[1][0:1, :], r32(onesc[:]), sqs[:, 0, :],
               start=(dc == 0), stop=(dc == DC - 1),
               reads=[onesc, sqs], writes=[bank[1]], tag="ln_sqsum")
        ts1("dve", rowA[:], bank[0][0:1, :], 1.0 / D, ALU.mult,
            reads=[bank[0]], writes=[rowA])
        tt("dve", rowB[:], rowA[:], rowA[:], ALU.mult, reads=[rowA],
           writes=[rowB])
        stt("dve", rowB[:], bank[1][0:1, :], 1.0 / D, rowB[:], ALU.mult,
            ALU.subtract, reads=[bank[1], rowB], writes=[rowB])
        act(rowB[:], rowB[:], AF.Sqrt, bias=epsb.h[:, :], reads=[rowB, epsb], writes=[rowB])
        mm(bank[2][:, :], ones1[:], rowA[:], start=True, stop=True,
           reads=[ones1, rowA], writes=[bank[2]])
        cp("dve", meanB[:], bank[2][:, :], reads=[bank[2]], writes=[meanB])
        recip(rowB[:], rowB[:], reads=[rowB], writes=[rowB])
        mm(bank[2][:, :], ones1[:], rowB[:], start=True, stop=True,
           reads=[ones1, rowB], writes=[bank[2]])
        cp("dve", rstdB[:], bank[2][:, :], reads=[bank[2]], writes=[rstdB])
        for dc in range(DC):
            sl = dc % 2
            dma("pool", xch[:, sl, :], xsrc.h[128 * dc:128 * dc + 128, c0:c0 + 512],
                reads=[xch, xsrc], writes=[xch], tag="ln_ldx2")
            tt("dve", t1s[:, sl, :], xch[:, sl, :].bitcast(F32), meanB[:], ALU.subtract,
               reads=[xch, meanB], writes=[t1s], tag="ln_t1")
            tt("dve", xn16[:, dc, :], t1s[:, sl, :], rstdB[:], ALU.mult,
               reads=[t1s, rstdB], writes=[xn16], tag="ln_app")
            if with_gate:
                cp("dve", t1hi[:, sl, :], t1s[:, sl, :], reads=[t1s],
                   writes=[t1hi], tag="t1hi")
                tt("dve", t1lo[:, sl, :], t1s[:, sl, :], t1hi[:, sl, :],
                   ALU.subtract, reads=[t1s, t1hi], writes=[t1lo], tag="t1lo")
                mm(bank[3][0:E, :], gwhi[:, dc, :], t1hi[:, sl, :],
                   start=(dc == 0), stop=False,
                   reads=[gwhi, t1hi], writes=[bank[3]], tag="gate_hh")
                mm(bank[3][0:E, :], gwlo[:, dc, :], t1hi[:, sl, :],
                   start=False, stop=False,
                   reads=[gwlo, t1hi], writes=[bank[3]], tag="gate_lh")
                mm(bank[3][0:E, :], gwhi[:, dc, :], t1lo[:, sl, :],
                   start=False, stop=(dc == DC - 1),
                   reads=[gwhi, t1lo], writes=[bank[3]], tag="gate_hl")
        if with_gate:
            tt("dve", lgr[:], bank[3][0:E, :], rstdB[0:E, :], ALU.mult,
               reads=[bank[3], rstdB], writes=[lgr], tag="gate_sc")

    def top2(tt_i):
        p.phase = "top2"
        if tap_en[0] and tt_i == 0 and "lgr0" in tapd:
            dma("sp", tapd["lgr0"].h[:, :], lgr[0:8, :], reads=[lgr])
        for j in range(4):
            tr(bank[4][:, 8 * j:8 * j + 8], lgr[:, 128 * j:128 * j + 128],
               eye[0:8, 0:8], start=(j == 0), reads=[lgr, eye],
               writes=[bank[4]])
        cp("dve", ltok[:], bank[4][:, 0:32], reads=[bank[4]], writes=[ltok])
        for b_ in range(4):
            lt = ltok[:, b_, :]
            rmax(m1t[:], lt, reads=[ltok], writes=[m1t])
            ts1("dve", m1n[:], m1t[:], -1.0, ALU.mult, reads=[m1t],
                writes=[m1n])
            ts1("dve", mskt[:], lt, m1t[:], ALU.is_equal, reads=[ltok, m1t],
                writes=[mskt])
            ts1("dve", wtmp[:], mskt[:], -1e30, ALU.mult, reads=[mskt],
                writes=[wtmp])
            tt("dve", wtmp[:], wtmp[:], lt, ALU.add, reads=[wtmp, ltok],
               writes=[wtmp])
            rmax(m2t[:], wtmp[:], reads=[wtmp], writes=[m2t])
            p.op("dve", lambda e: e.reduce_sum(cntt[:], mskt[:], axis=AX.X),
                 reads=[mskt], writes=[cntt])
            ts1("dve", cntt[:], cntt[:], 1.5, ALU.is_ge, reads=[cntt],
                writes=[cntt])
            tt("dve", dif1[:], m1t[:], m2t[:], ALU.subtract,
               reads=[m1t, m2t], writes=[dif1])
            tt("dve", dif1[:], dif1[:], cntt[:], ALU.mult,
               reads=[dif1, cntt], writes=[dif1])
            tt("dve", m2t[:], m2t[:], dif1[:], ALU.add, reads=[m2t, dif1],
               writes=[m2t])
            act(expt[:], lt, AF.Exp, bias=m1n[:], reads=[ltok, m1n],
                writes=[expt])
            act(exp2t[:], m2t[:], AF.Exp, bias=m1n[:], reads=[m2t, m1n],
                writes=[exp2t])
            ts1("dve", exp2t[:], exp2t[:], 1.0, ALU.add, reads=[exp2t],
                writes=[exp2t])
            recip(dinv[:], exp2t[:], reads=[exp2t], writes=[dinv])
            ts1("dve", mskt[:], lt, m2t[:], ALU.is_ge, reads=[ltok, m2t],
                writes=[mskt])
            tt("dve", wtmp[:], expt[:], mskt[:], ALU.mult,
               reads=[expt, mskt], writes=[wtmp])
            ts1("dve", wetok[:, b_, :], wtmp[:], dinv[:], ALU.mult,
                reads=[wtmp, dinv], writes=[wetok])
        if tap_en[0] and tt_i == 0 and "ltk0" in tapd:
            dma("sp", tapd["ltk0"].h[:, :],
                ltok.h[:, :, :].rearrange("p a b -> p (a b)"), reads=[ltok])
        if tap_en[0] and tt_i == 0 and "m1x" in tapd:
            dma("sp", tapd["m1x"].h[:, :], m1t[:], reads=[m1t])
            dma("sp", tapd["m2x"].h[:, :], m2t[:], reads=[m2t])
            dma("sp", tapd["dvx"].h[:, :], dinv[:], reads=[dinv])
        if tap_en[0] and "we" in tapd:
            dma("sp", tapd["we"].h[512 * tt_i:512 * tt_i + 512, :].rearrange(
                "(b p) e -> p b e", p=128), wetok[:], reads=[wetok])
        for j in range(4):
            tr(bank[4][0:E, 128 * j:128 * j + 128], wetok[:, j, :], eye[:],
               start=(j == 0), reads=[wetok, eye], writes=[bank[4]])
        cp("dve", weT[:], bank[4][0:E, :], reads=[bank[4]], writes=[weT])
        for e_ in range(E):
            mm(bank[4][:, :], sel8[:, 128 * e_:128 * e_ + 128],
               weT[0:8, :], start=True, stop=True,
               reads=[sel8, weT], writes=[bank[4]])
            cp("dve", weB[:, e_, :], bank[4][:, :], reads=[bank[4]],
               writes=[weB])

    def umm(tt_i):
        p.phase = "umm"
        for pre, adst, asrcd, ug in ((1, a1T, a1Td, u1g), (3, a3T, a3Td, u3g)):
            for g in range(3):
                if tt_i == 0:
                    dma("sp", adst[g][:, :, :],
                        asrcd.h[g].rearrange("p (dc n) -> p dc n", n=96),
                        reads=[adst[g], asrcd], writes=[adst[g]], tag="aT_ld")
                for dc in range(DC):
                    mm(bank[5][0:96, :], adst[g][:, dc, :], xn16[:, dc, :],
                       start=(dc == 0), stop=(dc == DC - 1),
                       reads=[adst[g], xn16], writes=[bank[5]], tag="umm")
                ts1("dve", ug[g][:, :], bank[5][0:96, :], 2.0, ALU.mult,
                    reads=[bank[5]], writes=[ug[g]], tag="uev")
        if tap_en[0] and tt_i == 0 and "a1t" in tapd:
            dma("sp", tapd["a1t"].h[:, :],
                a1T[0].h[:, :, :].rearrange("p a b -> p (a b)"), reads=[a1T[0]])
        if tap_en[0] and tt_i == 0 and "u1" in tapd:
            for g in range(3):
                dma("sp", tapd["u1"].h[96 * g:96 * g + 96, :], u1g[g][:, :],
                    reads=[u1g[g]])

    def ffn_phase_a(tt_i):
        p.phase = "ffa"
        phase_barrier(FFN_B_BUFS, FFN_A_BUFS, "ffb->ffa")
        for fc in range(FC):
            sl = fc % 2
            dma("act", w1Ts[:, sl, :, :], w1T.h[fc].rearrange("dc p f -> p dc f"),
                reads=[w1Ts, w1T], writes=[w1Ts], tag="w1T_ld")
            dma("sp", w3Ts[:, sl, :, :], w3T.h[fc].rearrange("dc p f -> p dc f"),
                reads=[w3Ts, w3T], writes=[w3Ts], tag="w3T_ld")
            for jb in range(3):
                ne = 3 if jb < 2 else 2
                dma("act", b1Ts[32 * jb:32 * jb + 16, sl, 3 * jb:3 * jb + ne, :],
                    b1T.h[:, 3 * jb:3 * jb + ne, 128 * fc:128 * fc + 128],
                    reads=[b1Ts, b1T], writes=[b1Ts], tag="b1T_ld")
                dma("sp", b3Ts[32 * jb:32 * jb + 16, sl, 3 * jb:3 * jb + ne, :],
                    b3T.h[:, 3 * jb:3 * jb + ne, 128 * fc:128 * fc + 128],
                    reads=[b3Ts, b3T], writes=[b3Ts], tag="b3T_ld")
            for dc in range(DC):
                mm(bank[0][:, :], w1Ts[:, sl, dc, :], xn16[:, dc, :],
                   start=(dc == 0), stop=(dc == DC - 1),
                   reads=[w1Ts, xn16], writes=[bank[0]], tag="base1")
            cp("dve", base1s[:], bank[0][:, :], reads=[bank[0]],
               writes=[base1s], tag="b1ev")
            for dc in range(DC):
                mm(bank[1][:, :], w3Ts[:, sl, dc, :], xn16[:, dc, :],
                   start=(dc == 0), stop=(dc == DC - 1),
                   reads=[w3Ts, xn16], writes=[bank[1]], tag="base3")
            cp("dve", base3s[:], bank[1][:, :], reads=[bank[1]],
               writes=[base3s], tag="b3ev")
            for e_ in range(E):
                g, j = e_ // 3, e_ % 3
                esl = e_ % 2
                h1b = bank[0] if esl == 0 else bank[2]
                h3b = bank[1] if esl == 0 else bank[7]
                mm(h1b[:, :], eyer[:], base1s[:], start=True,
                   stop=False, reads=[eyer, base1s], writes=[h1b],
                   tag="inj1")
                mm(h1b[:, :], b1Ts[:, sl, 3 * j + g, :],
                   u1g[g][0:96, :], start=False, stop=True,
                   reads=[b1Ts, u1g[g]], writes=[h1b], tag="d1")
                mm(h3b[:, :], eyer[:], base3s[:], start=True,
                   stop=False, reads=[eyer, base3s], writes=[h3b],
                   tag="inj3")
                mm(h3b[:, :], b3Ts[:, sl, 3 * j + g, :],
                   u3g[g][0:96, :], start=False, stop=True,
                   reads=[b3Ts, u3g[g]], writes=[h3b], tag="d3")
                act(ssil[:, esl, :], h1b[:, :], AF.Silu,
                    reads=[h1b], writes=[ssil], tag="silu")
                tt("dve", swt[:, esl, :], ssil[:, esl, :], weB[:, e_, :],
                   ALU.mult, reads=[ssil, weB], writes=[swt], tag="sw")
                tt("dve", zt[:, esl, :], swt[:, esl, :], h3b[:, :],
                   ALU.mult, reads=[swt, h3b], writes=[zt], tag="z")
                if e_ == 0:
                    cp("pool", accf[:], zt[:, esl, :], reads=[zt],
                       writes=[accf], tag="acc0")
                else:
                    tt("pool", accf[:], accf[:], zt[:, esl, :], ALU.add,
                       reads=[accf, zt], writes=[accf], tag="acc")
                if e_ > 0:
                    ep = e_ - 1
                    vb = bank[3 + ep // 2]
                    v0 = 32 * (ep % 2)
                    mm(vb[v0:v0 + 16, :], a2Ts[:, ep, R * fc:R * fc + R],
                       zt[:, ep % 2, :], start=(fc == 0), stop=(fc == FC - 1),
                       reads=[a2Ts, zt], writes=[vb], tag="vmm")
            mm(bank[6][32:48, :], a2Ts[:, 7, R * fc:R * fc + R],
               zt[:, 1, :], start=(fc == 0), stop=(fc == FC - 1),
               reads=[a2Ts, zt], writes=[bank[6]], tag="vmm")
            cp("dve", zsum[:, fc, :], accf[:], reads=[accf], writes=[zsum],
               tag="zsumev")
        for e_ in range(E):
            vb = bank[3 + e_ // 2]
            v0 = 32 * (e_ % 2)
            ts1("dve", vpw[0:16, e_, :], vb[v0:v0 + 16, :], 2.0, ALU.mult,
                reads=[vb], writes=[vpw], tag="vpw")
        if tap_en[0] and tt_i == 0 and "zs0" in tapd:
            dma("sp", tapd["zs0"].h.rearrange("(c p) t -> p c t", p=128),
                zsum[:], reads=[zsum])
        if tap_en[0] and tt_i == 0 and "vp0" in tapd:
            dma("sp", tapd["vp0"].h.rearrange("(c p) t -> p c t", p=16),
                vpw[:], reads=[vpw])

    def ffn_phase_b(tt_i, xsrc, xdst, tapx):
        p.phase = "ffb"
        phase_barrier(FFN_A_BUFS, FFN_B_BUFS, "ffa->ffb")
        c0 = 512 * tt_i
        for dh in range(2):
            for fc in range(FC):
                sl = fc % 2
                dma("act", w2Ts[:, sl, :, :],
                    w2T.h[fc, 8 * dh:8 * dh + 8].rearrange("dc p f -> p dc f"),
                    reads=[w2Ts, w2T], writes=[w2Ts], tag="w2T_ld")
                for dj in range(8):
                    mm(bank[dj][:, :], w2Ts[:, sl, dj, :], zsum[:, fc, :],
                       start=(fc == 0), stop=False,
                       reads=[w2Ts, zsum], writes=[bank[dj]], tag="outmm")
            for dq in range(2):
                dma("sp", b2Ts[:],
                    b2Td.h[:, :, 1024 * dh + 512 * dq:1024 * dh + 512 * dq + 512],
                    reads=[b2Ts, b2Td], writes=[b2Ts], tag="b2T_ld")
                for e_ in range(E):
                    for dj4 in range(4):
                        dj = 4 * dq + dj4
                        mm(bank[dj][:, :],
                           b2Ts[0:16, e_, 128 * dj4:128 * dj4 + 128],
                           vpw[0:16, e_, :], start=False,
                           stop=(e_ == E - 1), reads=[b2Ts, vpw],
                           writes=[bank[dj]], tag="b2mm")
            for dj in range(8):
                dc = 8 * dh + dj
                sl = dj % 2
                dma("pool", xch[:, sl, :],
                    xsrc.h[128 * dc:128 * dc + 128, c0:c0 + 512],
                    reads=[xch, xsrc], writes=[xch], tag="fb_ldx")
                stt("dve", ost[:, sl, :], bank[dj][:, :], 0.5,
                    xch[:, sl, :].bitcast(F32), ALU.mult, ALU.add, reads=[bank[dj], xch], writes=[ost],
                    tag="fb_res")
                dma("sp", xdst.h[128 * dc:128 * dc + 128, c0:c0 + 512],
                    ost[:, sl, :], reads=[ost], writes=[xdst], tag="fb_stx")
        if tapx is not None and tapx in tapd:
            for dc in range(DC):
                dma("sp", tapd[tapx].h[128 * dc:128 * dc + 128, c0:c0 + 512],
                    xdst.h[128 * dc:128 * dc + 128, c0:c0 + 512],
                    reads=[xdst])

    def qkv(tt_i):
        p.phase = "qkv"
        c0 = 512 * tt_i
        for fc in range(32):
            sl = fc % 2
            dma("pool", was[:, sl, :, :],
                Wat.h.rearrange("(c p) f -> p c f", p=128)[
                    :, :, 128 * fc:128 * fc + 128],
                reads=[was], writes=[was], tag="wa_ld")
            bk = bank[fc % 3]
            for dc in range(DC):
                mm(bk[:, :], was[:, sl, dc, :], xn16[:, dc, :],
                   start=(dc == 0), stop=(dc == DC - 1),
                   reads=[was, xn16], writes=[bk], tag="qk_mm")
            if fc < 16:
                p.op("dve", lambda e, o=qkst[:, sl, :], i=bk[:, :],
                     a=bavq[:, fc:fc + 1]: e.tensor_scalar(
                         out=o, in0=i, scalar1=0.125, scalar2=a,
                         op0=ALU.mult, op1=ALU.add),
                     reads=[bk, bavq], writes=[qkst], tag="q_ev")
            else:
                p.op("act", lambda e, o=qkst[:, sl, :], i=bk[:, :],
                     a=bav[:, fc:fc + 1]: e.add(out=o, in_=i, add=a),
                     reads=[bk, bav], writes=[qkst], tag="k_ev")
            dstd = qfd if fc < 16 else kfd
            fcl = fc % 16
            dma("sp", dstd.h[128 * fcl:128 * fcl + 128, c0:c0 + 512],
                qkst[:, sl, :], reads=[qkst], writes=[dstd], tag="qk_st")
            if fc >= 16 and tt_i == NTT - 1:
                dma("sp", kbin.h[128 * fcl:128 * fcl + 128, :],
                    qkst[:, sl, 128:512], reads=[qkst], writes=[kbin],
                    tag="kb_st")
        for tb in range(4):
            tbg = 4 * tt_i + tb
            for vs in range(4):
                dma("pool", wvs[:, :, :],
                    Wat.h.rearrange("(c p) f -> p c f", p=128)[
                        :, :, 4096 + 512 * vs:4096 + 512 * vs + 512],
                    reads=[wvs], writes=[wvs], tag="wv_ld")
                bk = bank[3 + vs % 2]
                for dc in range(DC):
                    mm(bk[:, :], xn16[:, dc, 128 * tb:128 * tb + 128],
                       wvs[:, dc, :], start=(dc == 0), stop=False,
                       reads=[wvs, xn16], writes=[bk], tag="v_mm")
                mm(bk[:, :], r32(ones1[:]).bitcast(F32R) if False else
                   ones1h[:], vbias[0:1, vs, :], start=False, stop=True,
                   reads=[ones1h, vbias], writes=[bk], tag="v_bias")
                cp("dve",
                   vstage.h.rearrange("p (h v) -> p h v", v=VP)[
                       :, 8 * vs:8 * vs + 8, 0:64],
                   bk.h[:, :].rearrange("p (h v) -> p h v", v=64),
                   reads=[bk], writes=[vstage], tag="v_ev")
            dma("sp", vtd.h[128 * tbg:128 * tbg + 128, :], vstage[:],
                reads=[vstage], writes=[vtd], tag="v_st")
            if tbg >= NTB - 3:
                dma("sp", vbin.h[128 * (tbg - (NTB - 3)):
                                 128 * (tbg - (NTB - 3)) + 128, :],
                    vstage[:], reads=[vstage], writes=[vbin], tag="vb_st")

    def boundary():
        p.phase = "bnd"
        phase_barrier([vstage, pP, rsb, att], [gstage, gstage2], "qkv->bnd")
        from concourse.bass_core_rust import IndirectOffsetOnAxis as IOA
        p.op("pool", lambda e: e.collective_compute(
            "AllGather", ALU.bypass, [list(range(8))],
            [kbin.h.rearrange("(a r) t -> a (r t)", a=16).opt()],
            [kbg.h[:, :].opt()]), reads=[kbin], writes=[kbg], cc=True)
        p.op("pool", lambda e: e.collective_compute(
            "AllGather", ALU.bypass, [list(range(8))],
            [vbin.h.rearrange("(a r) t -> a (r t)", a=16).opt()],
            [vbg.h[:, :].opt()]), reads=[vbin], writes=[vbg], cc=True)
        p.op("pool", lambda e: e.indirect_dma_start(
            gstage[:, :], None, kbg.h.rearrange("a (b t) -> (a b) t", b=8),
            IOA(gidx[:], 0)), reads=[kbg, gidx, gstage], writes=[gstage],
            dma=True, tag="kb_g")
        dma("sp", kbp.h.rearrange("(a i) t -> a (i t)", a=128),
            gstage[:, :], reads=[gstage], writes=[kbp], tag="kb_w")
        p.op("pool", lambda e: e.indirect_dma_start(
            gstage2[:, :], None, vbg.h.rearrange("a (b t) -> (a b) t", b=8),
            IOA(gidx[:], 0)), reads=[vbg, gidx, gstage2],
            writes=[gstage2], dma=True, tag="vb_g")
        dma("sp", vbp.h.rearrange("(a i) t -> a (i t)", a=128),
            gstage2[:, :], reads=[gstage2], writes=[vbp], tag="vb_w")

    def attention(qt, xsrc, xdst):
        p.phase = "att"
        c0 = 512 * qt
        dma("act", qw[:, :, :], qfd.h.rearrange("(c p) t -> p c t", p=128)[
            :, :, c0:c0 + 512], reads=[qw, qfd], writes=[qw], tag="qw_ld")
        for kc in range(NKC):
            kl0 = 512 * qt - WIN + 128 * kc
            if kl0 < 0:
                dma("act", kw[:, :, 128 * kc:128 * kc + 128],
                    kbp.h.rearrange("(c p) t -> p c t", p=128)[
                        :, :, WIN + kl0:WIN + kl0 + 128],
                    reads=[kw, kfd, kbp], writes=[kw], tag="kw_ld")
                dma("sp", vw[:, kc, :],
                    vbp.h.rearrange("(b p) c -> p b c", p=128)[
                        :, (WIN + kl0) // 128, :],
                    reads=[vw, vtd, vbp], writes=[vw], tag="vw_ld")
            else:
                dma("act", kw[:, :, 128 * kc:128 * kc + 128],
                    kfd.h.rearrange("(c p) t -> p c t", p=128)[
                        :, :, kl0:kl0 + 128],
                    reads=[kw, kfd, kbp], writes=[kw], tag="kw_ld")
                dma("sp", vw[:, kc, :],
                    vtd.h.rearrange("(b p) c -> p b c", p=128)[
                        :, kl0 // 128, :],
                    reads=[vw, vtd, vbp], writes=[vw], tag="vw_ld")
        for hp in range(16):
            for h2 in range(2):
                h_ = 2 * hp + h2
                r0 = 64 * h2
                pvb = bank[2 + h2]
                for kc in range(NKC):
                    sbk = bank[h2] if kc % 2 == 0 else bank[5 + h2]
                    psl = 2 * h2 + kc % 2
                    mm(sbk[:, :], kw[r0:r0 + 64, hp, 128 * kc:128 * kc + 128],
                       qw[r0:r0 + 64, hp, :], start=True, stop=False,
                       reads=[kw, qw], writes=[sbk], tag="qk")
                    mm(sbk[:, :], ones1h[:], aq[0:1, qt, :],
                       start=False, stop=True, reads=[ones1h, aq],
                       writes=[sbk], tag="alibi")
                    if kc >= 3:
                        tt("dve", sbk[:, :], sbk[:, :], masks[:, kc - 3, :],
                           ALU.add, reads=[sbk, masks], writes=[sbk],
                           tag="mask")
                    act(pP[:, psl, :], sbk[:, :], AF.Exp,
                        bias=biask[:, NKC * qt + kc:NKC * qt + kc + 1],
                        reads=[sbk, biask], writes=[pP], tag="exp")
                    mm(pvb[0:VP, :], vw[:, kc, VP * h_:VP * h_ + VP],
                       pP[:, psl, :], start=(kc == 0), stop=(kc == NKC - 1),
                       reads=[vw, pP], writes=[pvb], tag="pv")
                recip((rsiA if h2 == 0 else rsiB)[:], pvb[64:65, :],
                      reads=[pvb], writes=[rsiA if h2 == 0 else rsiB],
                      tag="rsinv")
            mm(bank[4][:, :], indA[:], rsiA[:], start=True,
               stop=False, reads=[indA, rsiA], writes=[bank[4]], tag="rsb1")
            mm(bank[4][:, :], indB[:], rsiB[:], start=False,
               stop=True, reads=[indB, rsiB], writes=[bank[4]], tag="rsb2")
            cp("dve", rsb[:], bank[4][:, :], reads=[bank[4]], writes=[rsb])
            tt("dve", att[0:64, hp, :], bank[2][0:64, :], rsb[0:64, :],
               ALU.mult, reads=[bank[2], rsb], writes=[att], tag="nrmA")
            tt("dve", att[64:128, hp, :], bank[3][0:64, :], rsb[64:128, :],
               ALU.mult, reads=[bank[3], rsb], writes=[att], tag="nrmB")
        if qt == 0 and "at0" in tapd:
            dma("sp", tapd["at0"].h.rearrange("(c p) t -> p c t", p=128),
                att[:], reads=[att])
        for dc in range(DC):
            sl = dc % 2
            dma("pool", was[:, sl, :, :],
                Wpr.h.rearrange("(c p) f -> p c f", p=128)[
                    :, :, 128 * dc:128 * dc + 128],
                reads=[was], writes=[was], tag="wp_ld")
            bk = bank[5 + dc % 2]
            for fc in range(DC):
                mm(bk[:, :], was[:, sl, fc, :], att[:, fc, :],
                   start=(fc == 0), stop=(fc == DC - 1),
                   reads=[was, att], writes=[bk], tag="pr_mm")
            dma("pool", xch[:, sl, :],
                xsrc.h[128 * dc:128 * dc + 128, c0:c0 + 512],
                reads=[xch, xsrc], writes=[xch], tag="pr_ldx")
            stt("dve", ost[:, sl, :], bk[:, :], bpv[:, dc:dc + 1],
                xch[:, sl, :].bitcast(F32), ALU.add, ALU.add,
                reads=[bk, bpv, xch], writes=[ost], tag="pr_res")
            dma("sp", xdst.h[128 * dc:128 * dc + 128, c0:c0 + 512],
                ost[:, sl, :], reads=[ost], writes=[xdst], tag="pr_stx")

    phase_barrier(CACHE_BUFS, FFN_CORE + FFN_A_BUFS + FFN_B_BUFS, "cache->ffn")

    tap_en = [True]

    def ffn_sublayer(xsrc, xdst, tapx):
        memset(b1Ts, 0.0)
        memset(b3Ts, 0.0)
        dma("sp", gwT.h[:, :, :],
            gwTd.h.rearrange("p (dc e) -> p dc e", e=E), reads=[gwT, gwTd],
            writes=[gwT], tag="gwT_ld")
        dma("sp", sel8[:], sel_d.h[:, :], reads=[sel8], writes=[sel8],
            tag="sel_ld")
        for e_ in range(E):
            dma("sp", a2Ts[:, e_, :], a2Td.h[e_, :, :],
                reads=[a2Ts, a2Td], writes=[a2Ts], tag="a2s_ld")
        cp("dve", gwhi.h[:, :, :], gwT.h[:, :, :], reads=[gwT], writes=[gwhi])
        tt("dve", gwlo.h[:, :, :], gwT.h[:, :, :], gwhi.h[:, :, :],
           ALU.subtract, reads=[gwT, gwhi], writes=[gwlo])
        if tap_en[0] and "gwt" in tapd:
            dma("sp", tapd["gwt"].h[:, :],
                gwT.h[:, :, :].rearrange("p a b -> p (a b)"), reads=[gwT])
        for tt_i in range(NTT):
            layernorm(xsrc, tt_i, with_gate=True)
            if tt_i == 0 and xsrc is xin and "ln1" in tapd:
                dma("sp", tapd["ln1"].h.rearrange("(c p) t -> p c t", p=128)[
                    :, :, 0:512], xn16[:], reads=[xn16])
            top2(tt_i)
            umm(tt_i)
            ffn_phase_a(tt_i)
            ffn_phase_b(tt_i, xsrc, xdst, tapx)

    ffn_sublayer(xin, x1d, "x1")

    phase_barrier(FFN_CORE + FFN_A_BUFS + FFN_B_BUFS, ATTN_BUFS, "ffn->attn")
    memset(vstage, 0.0)
    memset(vstage, 1.0,
           vstage.h.rearrange("p (h v) -> p h v", v=VP)[:, :, 64:65])
    for tt_i in range(NTT):
        layernorm(x1d, tt_i, with_gate=False)
        qkv(tt_i)
    if "qf" in tapd:
        for dc in range(DC):
            dma("sp", tapd["qf"].h[128 * dc:128 * dc + 128, :],
                qfd.h[128 * dc:128 * dc + 128, :], reads=[qfd])
    if "kf" in tapd:
        for dc in range(DC):
            dma("sp", tapd["kf"].h[128 * dc:128 * dc + 128, :],
                kfd.h[128 * dc:128 * dc + 128, :], reads=[kfd])
    if "vt" in tapd:
        for tb in range(NTB):
            dma("sp", tapd["vt"].h[128 * tb:128 * tb + 128, :],
                vtd.h[128 * tb:128 * tb + 128, :], reads=[vtd])
    boundary()
    phase_barrier([gstage, gstage2], [pP, rsb, att], "bnd->attc")
    for qt in range(NTT):
        attention(qt, x1d, x2d)
    if "x2" in tapd:
        for dc in range(DC):
            dma("sp", tapd["x2"].h[128 * dc:128 * dc + 128, :],
                x2d.h[128 * dc:128 * dc + 128, :], reads=[x2d])

    phase_barrier(ATTN_BUFS, FFN_CORE + FFN_A_BUFS + FFN_B_BUFS, "attn->ffn")
    tap_en[0] = False
    ffn_sublayer(x2d, xout, None)

    p.finalize(block, sems, dbg=True)
    ctx.close()
    return nc



def _consts(core):
    eye = np.eye(128, dtype=np.float32)
    mask = np.zeros((4, 128, 512), dtype=np.float32)
    for j in range(4):
        for kp in range(128):
            q0 = 128 * j + kp
            if q0 > 0:
                mask[j, kp, 0:min(q0, 512)] = -30000.0
    bik = np.zeros((NTT * NKC, 128), dtype=np.float32)
    for qt in range(NTT):
        for kc in range(NKC):
            kloc = 512 * qt - WIN + 128 * kc + np.arange(128)
            v = MSLOPE * kloc + CSHIFT
            if core == 0:
                v = np.where(kloc < 0, -30000.0, v)
            bik[NKC * qt + kc] = v
    aqv = np.zeros((NTT, 512), dtype=np.float32)
    for qt in range(NTT):
        aqv[qt] = -MSLOPE * (512 * qt + np.arange(512))
    inda = np.zeros((1, 128), dtype=np.float32)
    inda[0, 0:64] = 1.0
    indb = np.zeros((1, 128), dtype=np.float32)
    indb[0, 64:128] = 1.0
    sel8 = np.zeros((8, 8 * 128), dtype=np.float32)
    for e_ in range(8):
        sel8[e_, 128 * e_:128 * e_ + 128] = 1.0
    gidx = (128 * ((core - 1) % 8) + np.arange(128)).astype(np.int32)
    return dict(eyec=eye, eyerc=eye, eye2c=(2.0 * eye).astype(np.float32),
                maskc=mask, biaskc=np.ascontiguousarray(bik.T), aqc=aqv.reshape(1, NTT, 512),
                indac=inda, indbc=indb, sel8c=sel8,
                gidxc=gidx.reshape(128, 1))


_taps = ()


def kernel(L0_attn_hidden_states, L0_attn_W_attn, L0_attn_b_attn,
           L0_attn_W_proj, L0_attn_b_proj, L0_ffn_gate_w, L0_ffn_w1,
           L0_ffn_w3, L0_ffn_w2, L0_ffn_a1, L0_ffn_b1, L0_ffn_a3, L0_ffn_b3,
           L0_ffn_a2, L0_ffn_b2):
    x = np.asarray(L0_attn_hidden_states, dtype=np.float32).reshape(S, D)
    shared = {
        "Wat": np.ascontiguousarray(np.asarray(L0_attn_W_attn, np.float32)),
        "bat": np.ascontiguousarray(
            np.asarray(L0_attn_b_attn, np.float32).reshape(48, 128)),
        "batT": np.ascontiguousarray(
            np.asarray(L0_attn_b_attn, np.float32).reshape(48, 128).T),
        "bprT": np.ascontiguousarray(
            np.asarray(L0_attn_b_proj, np.float32).reshape(16, 128).T),
        "Wpr": np.ascontiguousarray(np.asarray(L0_attn_W_proj, np.float32)),
        "bpr": np.ascontiguousarray(
            np.asarray(L0_attn_b_proj, np.float32).reshape(16, 128)),
        "gw": np.ascontiguousarray(np.asarray(L0_ffn_gate_w, np.float32)),
        "w1": np.ascontiguousarray(np.asarray(L0_ffn_w1, np.float32)),
        "w3": np.ascontiguousarray(np.asarray(L0_ffn_w3, np.float32)),
        "w2": np.ascontiguousarray(np.asarray(L0_ffn_w2, np.float32)),
        "a1": np.ascontiguousarray(np.asarray(L0_ffn_a1, np.float32)),
        "b1": np.ascontiguousarray(np.asarray(L0_ffn_b1, np.float32)),
        "a3": np.ascontiguousarray(np.asarray(L0_ffn_a3, np.float32)),
        "b3": np.ascontiguousarray(np.asarray(L0_ffn_b3, np.float32)),
        "a2": np.ascontiguousarray(np.asarray(L0_ffn_a2, np.float32)),
        "b2": np.ascontiguousarray(np.asarray(L0_ffn_b2, np.float32)),
    }
    in_maps = []
    for c in range(NCORES):
        m = dict(shared)
        m["xT"] = np.ascontiguousarray(x[TPC * c:TPC * (c + 1), :].T)
        m.update(_consts(c))
        in_maps.append(m)
    nc = bass.Bass("TRN2", target_bir_lowering=False)
    build_program(nc, taps=_taps)
    res = run_bass_kernel_spmd(nc, in_maps, core_ids=list(range(NCORES)))
    out = np.concatenate([r["xo"].T for r in res.results], axis=0)
    return out.reshape(1, S, D).astype(np.float32)


# version 06
# speedup vs baseline: 1.1641x; 1.1641x over previous
import numpy as np
from contextlib import ExitStack

import concourse.bass as bass
import concourse.mybir as mybir
from concourse.bass_utils import run_bass_kernel_spmd

F32 = mybir.dt.float32
F32R = mybir.dt.float32r
BF16 = mybir.dt.bfloat16
I32 = mybir.dt.int32
AF = mybir.ActivationFunctionType
ALU = mybir.AluOpType
AX = mybir.AxisListType

NCORES = 8
S, D, FF, E, R, H, HD = 8192, 2048, 5632, 8, 16, 32, 64
TPC = S // NCORES
DC = D // 128
FC = FF // 128
NTT = TPC // 512
NTB = TPC // 128
WIN = 384
NKC = 7
VP = HD + 1
HVP = H * VP
MSLOPE = 2.0 ** (-0.25)
CSHIFT = -16.0
EPS = 1e-5

ENGINES = ("sp", "pe", "dve", "act", "pool")
BLOCK_OF = {"sp": "sync", "pe": "tensor", "dve": "vector", "act": "scalar",
            "pool": "gpsimd"}
DMA_SEM = {"sp": "dS", "pool": "dP", "act": "dA"}
ESZ = {F32: 4, F32R: 4, BF16: 2, I32: 4}


class Buf:
    __slots__ = ("h", "name")

    def __init__(self, h, name):
        self.h = h
        self.name = name

    def __getitem__(self, idx):
        return self.h[idx]


SERIAL = [False]
DMA_FIFO = [True]
DMA_W = {"sp": 1, "pool": 1, "act": 1}
PE_SYNC = [False]
SERIAL_PHASES = set()
SERIAL_TAGS = set()


class Prog:
    def __init__(self, nc):
        self.nc = nc
        self.ops = []
        self.last_w = {}
        self.readers = {}
        self.prev = None

    phase = "init"

    def op(self, engine, fn, reads=(), writes=(), dma=False, cc=False, tag=""):
        oid = len(self.ops)
        deps = set()
        for b in reads:
            deps.update(self.last_w.get(b.name, {}).values())
        for b in writes:
            deps.update(self.last_w.get(b.name, {}).values())
            deps.update(self.readers.get(b.name, {}).values())
        if (SERIAL[0] or self.phase in SERIAL_PHASES or tag in SERIAL_TAGS) and self.prev is not None:
            deps.add(self.prev)
        self.ops.append(dict(id=oid, engine=engine, fn=fn, dma=dma, cc=cc,
                             deps=deps, tag=tag))
        self.prev = oid
        key = engine + ("D" if dma else "")
        for b in reads:
            self.readers.setdefault(b.name, {})[key] = oid
        for b in writes:
            self.last_w[b.name] = {key: oid}
            self.readers[b.name] = {}
        return oid

    def barrier(self, eng_fn, reads, writes, tag="barrier"):
        self.op("dve", eng_fn, reads=reads, writes=writes, tag=tag)

    def finalize(self, block, sems, dbg=False):
        ops = self.ops
        dma_cum = {"sp": 0, "pool": 0, "act": 0}
        cc_cum = 0
        for o in ops:
            if o["dma"]:
                dma_cum[o["engine"]] += 16
                o["sem"], o["val"], o["inc"] = (DMA_SEM[o["engine"]],
                                                dma_cum[o["engine"]], 16)
            elif o["cc"]:
                cc_cum += 1
                o["sem"], o["val"], o["inc"] = "cc", cc_cum, 1
        needed = set()
        for o in ops:
            for pid in o["deps"]:
                po = ops[pid]
                if po["dma"] or po["cc"]:
                    continue
                if (not SERIAL[0] and not PE_SYNC[0]
                        and po["engine"] == "pe" and o["engine"] == "pe"):
                    continue
                needed.add(pid)
        rank = {e: 0 for e in ENGINES}
        for o in ops:
            if o["dma"] or o["cc"]:
                continue
            if o["id"] in needed:
                rank[o["engine"]] += 1
                o["sem"], o["val"], o["inc"] = o["engine"], rank[o["engine"]], 1
            else:
                o["sem"] = None
        hw = {}
        n_waits = 0
        dma_hist = {"sp": [], "pool": [], "act": []}
        for o in ops:
            w = {}
            if DMA_FIFO[0] and o["dma"]:
                hist = dma_hist[o["engine"]]
                wd = DMA_W[o["engine"]]
                if len(hist) >= wd:
                    w[DMA_SEM[o["engine"]]] = hist[-wd]
                hist.append(o["val"])
            for pid in o["deps"]:
                po = ops[pid]
                if po.get("sem") is None:
                    continue
                if (not SERIAL[0] and not PE_SYNC[0]
                        and not po["dma"] and not po["cc"]
                        and po["engine"] == "pe" and o["engine"] == "pe"):
                    continue
                w[po["sem"]] = max(w.get(po["sem"], 0), po["val"])
            real = {}
            for sname, v in sorted(w.items()):
                if hw.get((o["engine"], sname), 0) >= v:
                    continue
                real[sname] = v
                hw[(o["engine"], sname)] = v
            o["waits"] = real
            n_waits += len(real)
        if dbg:
            cnt = {}
            for o in ops:
                cnt[o["engine"]] = cnt.get(o["engine"], 0) + 1
            print(f"[prog] ops={len(ops)} waits={n_waits} per-engine={cnt} "
                  f"comp-ranks={rank} dma16={dma_cum} cc={cc_cum}", flush=True)

        def mk(engine):
            def body(eng):
                for o in ops:
                    if o["engine"] != engine:
                        continue
                    for sname, v in o["waits"].items():
                        eng.wait_ge(sems[sname], v)
                    try:
                        inst = o["fn"](eng)
                    except Exception as ex:
                        raise RuntimeError(
                            f"emit failed op={o['id']} tag={o['tag']}: {ex}")
                    if o.get("sem") is not None:
                        inst.then_inc(sems[o["sem"]], o["inc"])
                if engine == "sp":
                    for q, tot in dma_cum.items():
                        if tot:
                            eng.wait_ge(sems[DMA_SEM[q]], tot)
                    for e2 in ("pe", "dve", "act", "pool"):
                        if rank[e2]:
                            eng.wait_ge(sems[e2], rank[e2])
            return body

        for e in ENGINES:
            getattr(block, BLOCK_OF[e])(mk(e))


def _prod(sh):
    r = 1
    for s in sh:
        r *= s
    return r



def build_program(nc, taps=()):
    taps = set(taps)
    ctx = ExitStack()

    def dram(name, shape, dtype, kind="Internal"):
        return Buf(nc.dram_tensor(name, list(shape), dtype, kind=kind).ap(),
                   name)

    def sbuf(name, shape, dtype):
        return Buf(ctx.enter_context(nc.sbuf_tensor(name, list(shape), dtype)),
                   name)

    def psum(name, shape, dtype=F32):
        return Buf(ctx.enter_context(nc.psum_tensor(name, list(shape), dtype)),
                   name)

    xin = dram("xT", (D, TPC), F32R, "ExternalInput")
    Wat = dram("Wat", (D, 3 * D), F32, "ExternalInput")
    bat = dram("bat", (48, 128), F32, "ExternalInput")
    batT = dram("batT", (128, 48), F32, "ExternalInput")
    bprT = dram("bprT", (128, 16), F32, "ExternalInput")
    Wpr = dram("Wpr", (D, D), F32, "ExternalInput")
    bpr = dram("bpr", (16, 128), F32, "ExternalInput")
    gw_d = dram("gw", (E, D), F32, "ExternalInput")
    w1_d = dram("w1", (FF, D), F32, "ExternalInput")
    w3_d = dram("w3", (FF, D), F32, "ExternalInput")
    w2_d = dram("w2", (D, FF), F32, "ExternalInput")
    a1_d = dram("a1", (E, R, D), F32, "ExternalInput")
    b1_d = dram("b1", (E, FF, R), F32, "ExternalInput")
    a3_d = dram("a3", (E, R, D), F32, "ExternalInput")
    b3_d = dram("b3", (E, FF, R), F32, "ExternalInput")
    a2_d = dram("a2", (E, R, FF), F32, "ExternalInput")
    b2_d = dram("b2", (E, D, R), F32, "ExternalInput")
    eye_d = dram("eyec", (128, 128), F32, "ExternalInput")
    eyer_d = dram("eyerc", (128, 128), F32R, "ExternalInput")
    eye2_d = dram("eye2c", (128, 128), F32, "ExternalInput")
    mask_d = dram("maskc", (4, 128, 512), F32, "ExternalInput")
    bik_d = dram("biaskc", (128, NTT * NKC), F32, "ExternalInput")
    aq_d = dram("aqc", (1, NTT, 512), F32, "ExternalInput")
    inda_d = dram("indac", (1, 128), F32, "ExternalInput")
    indb_d = dram("indbc", (1, 128), F32, "ExternalInput")
    sel_d = dram("sel8c", (8, 8 * 128), F32, "ExternalInput")
    gix_d = dram("gidxc", (128, 1), I32, "ExternalInput")

    xout = dram("xo", (D, TPC), F32R, "ExternalOutput")

    x1d = dram("x1d", (D, TPC), F32R)
    x2d = dram("x2d", (D, TPC), F32R)
    w1T = dram("w1T", (FC, DC, 128, 128), BF16)
    w3T = dram("w3T", (FC, DC, 128, 128), BF16)
    w2T = dram("w2T", (FC, DC, 128, 128), BF16)
    b1T = dram("b1T", (16, E, FF), BF16)
    b3T = dram("b3T", (16, E, FF), BF16)
    a2Td = dram("a2Td", (E, 128, FC * R), BF16)
    b2Td = dram("b2Td", (16, E, D), BF16)
    gwTd = dram("gwTd", (128, DC * E), F32)
    a1Td = dram("a1Td", (3, 128, DC * 96), BF16)
    a3Td = dram("a3Td", (3, 128, DC * 96), BF16)
    qfd = dram("qfd", (D, TPC), BF16)
    kfd = dram("kfd", (D, TPC), BF16)
    vtd = dram("vtd", (TPC, HVP), BF16)
    kbin = dram("kbin", (D, WIN), BF16)
    vbin = dram("vbin", (WIN, HVP), BF16)
    kbg = dram("kbg", (8 * 16, D * WIN // 16), BF16)
    vbg = dram("vbg", (8 * 16, WIN * HVP // 16), BF16)
    kbp = dram("kbp", (D, WIN), BF16)
    vbp = dram("vbp", (WIN, HVP), BF16)

    eye = sbuf("eye", (128, 128), F32)
    eyer = sbuf("eyer", (128, 128), F32R)
    eye16 = sbuf("eye16", (128, 128), BF16)
    eye16x2 = sbuf("eye16x2", (128, 128), BF16)
    ones1 = sbuf("ones1", (1, 128), F32)
    ones1h = sbuf("ones1h", (1, 128), BF16)
    onesc = sbuf("onesc", (128, 1), F32)
    masks = sbuf("masks", (128, 4, 512), BF16)
    biask = sbuf("biask", (128, NTT * NKC), F32)
    aq = sbuf("aq", (1, NTT, 512), BF16)
    indA = sbuf("indA", (1, 128), F32)
    indB = sbuf("indB", (1, 128), F32)
    gidx = sbuf("gidx", (128, 1), I32)
    bav = sbuf("bav", (128, 48), F32)
    bavq = sbuf("bavq", (128, 16), F32)
    vbias = sbuf("vbias", (1, 4, 512), BF16)
    bpv = sbuf("bpv", (128, 16), F32)
    xch = sbuf("xch", (128, 2, 512), F32R)
    base1s = sbuf("base1s", (128, 512), F32R)
    base3s = sbuf("base3s", (128, 512), F32R)
    ost = sbuf("ost", (128, 2, 512), F32R)
    sqs = sbuf("sqs", (128, 1, 512), F32R)
    t1s = sbuf("t1s", (128, 2, 512), F32)
    xn16 = sbuf("xn16", (128, DC, 512), BF16)
    rowA = sbuf("rowA", (1, 512), F32)
    rowB = sbuf("rowB", (1, 512), F32)
    meanB = sbuf("meanB", (128, 512), F32)
    rstdB = sbuf("rstdB", (128, 512), F32)
    qkst = sbuf("qkst", (128, 2, 512), BF16)
    epsb = sbuf("epsb", (1, 1), F32)
    m1t = sbuf("m1t", (128, 1), F32)
    m1n = sbuf("m1n", (128, 1), F32)
    m2t = sbuf("m2t", (128, 1), F32)
    cntt = sbuf("cntt", (128, 1), F32)
    dif1 = sbuf("dif1", (128, 1), F32)
    expt = sbuf("expt", (128, E), F32)
    exp2t = sbuf("exp2t", (128, 1), F32)
    dinv = sbuf("dinv", (128, 1), F32)
    mskt = sbuf("mskt", (128, E), F32)
    wtmp = sbuf("wtmp", (128, E), F32)
    ltok = sbuf("ltok", (128, 4, E), F32)
    wetok = sbuf("wetok", (128, 4, E), F32)

    AR_BYTES = 149 * 1024
    arena = sbuf("AR", (128, AR_BYTES // 2), BF16)
    ar32 = arena.h[:, :].bitcast(F32)
    ar16 = arena.h[:, :]

    def carve(name, byte_off, parts, shape, dtype):
        n = _prod(shape)
        esz = ESZ[dtype]
        assert byte_off % 64 == 0
        assert byte_off + n * esz <= AR_BYTES, (name, byte_off, n * esz)
        base = ar16 if dtype == BF16 else arena.h[:, :].bitcast(dtype)
        v = base[0:parts, byte_off // esz: byte_off // esz + n]
        if len(shape) > 1:
            spec = "p (" + " ".join(f"a{i}" for i in range(len(shape))) + ") -> p " + \
                   " ".join(f"a{i}" for i in range(len(shape)))
            kw = {f"a{i}": shape[i] for i in range(len(shape))}
            v = v.rearrange(spec, **kw)
        return Buf(v, name)

    def align(x):
        return (x + 63) & ~63

    off = 0

    def take(name, parts, shape, dtype):
        nonlocal off
        b = carve(name, off, parts, shape, dtype)
        off = align(off + _prod(shape) * ESZ[dtype])
        return b

    off = 0
    zsum = take("zsum", 128, (FC, 512), BF16)
    weB = take("weB", 128, (E, 512), BF16)
    vpw = take("vpw", 16, (E, 512), BF16)
    accf = take("accf", 128, (512,), F32)
    ssil = take("ssil", 128, (2, 512), BF16)
    swt = take("swt", 128, (2, 512), BF16)
    zt = take("zt", 128, (2, 512), BF16)
    a2Ts = take("a2Ts", 128, (E, FC * R), BF16)
    lgr = take("lgr", 8, (512,), F32)
    weT = take("weT", 8, (512,), F32)
    u1g = [take(f"u1g{g}", 96, (512,), BF16) for g in range(3)]
    u3g = [take(f"u3g{g}", 96, (512,), BF16) for g in range(3)]
    a1T = [take(f"a1T{g}", 128, (DC, 96), BF16) for g in range(3)]
    a3T = [take(f"a3T{g}", 128, (DC, 96), BF16) for g in range(3)]
    gwT = take("gwT", 128, (DC, E), F32)
    gwhi = take("gwhi", 128, (DC, E), BF16)
    gwlo = take("gwlo", 128, (DC, E), BF16)
    t1hi = take("t1hi", 128, (2, 512), BF16)
    t1lo = take("t1lo", 128, (2, 512), BF16)
    sel8 = take("sel8", 8, (8 * 128,), F32)
    ffn_stream_off = off
    off = ffn_stream_off
    w1Ts = take("w1Ts", 128, (2, DC, 128), BF16)
    w3Ts = take("w3Ts", 128, (2, DC, 128), BF16)
    b1Ts = take("b1Ts", 96, (2, E, 128), BF16)
    b3Ts = take("b3Ts", 96, (2, E, 128), BF16)
    ffn_a_end = off
    off = ffn_stream_off
    w2Ts = take("w2Ts", 128, (2, 8, 128), BF16)
    b2Ts = take("b2Ts", 16, (E, 512), BF16)
    ffn_b_end = off
    ffn_end = max(ffn_a_end, ffn_b_end)
    assert ffn_end <= AR_BYTES, ffn_end

    off = 0
    wnat = take("wnat", 128, (2, D), BF16)
    wtt = take("wtt", 128, (2, DC, 128), BF16)
    bnat = take("bnat", 128, (FC, R), BF16)
    a2nat = take("a2nat", 16, (FF,), BF16)
    b2nat = take("b2nat", 128, (DC, R), BF16)
    tev = take("tev", 16, (FC * 128,), BF16)
    tev2 = take("tev2", 128, (DC * 96,), BF16)
    tev3 = take("tev3", 16, (D,), BF16)
    apad = take("apad", 96, (D,), F32)
    gws = take("gws", 8, (D,), F32)
    gwstg = take("gwstg", 128, (DC * E,), F32)
    cache_end = off
    assert cache_end <= AR_BYTES

    off = 0
    qw = take("qw", 128, (16, 512), BF16)
    kw = take("kw", 128, (16, NKC * 128), BF16)
    vw = take("vw", 128, (NKC, HVP), BF16)
    was = take("was", 128, (2, DC, 128), BF16)
    wvs = take("wvs", 128, (DC, 512), BF16)
    vstage = take("vstage", 128, (HVP,), BF16)
    rsiA = take("rsiA", 1, (512,), F32)
    rsiB = take("rsiB", 1, (512,), F32)
    core_off = off
    pP = take("pP", 128, (4, 512), BF16)
    rsb = take("rsb", 128, (512,), F32)
    att = take("att", 128, (16, 512), BF16)
    attn_end = off
    off = core_off
    gstage = take("gstage", 128, (6144,), BF16)
    gstage2 = take("gstage2", 128, (6240,), BF16)
    attn_end = max(attn_end, off)
    assert attn_end <= AR_BYTES, attn_end

    CACHE_BUFS = [wnat, wtt, bnat, a2nat, b2nat, tev, tev2, tev3, apad, gws]
    FFN_CORE = ([zsum, weB, vpw, accf, ssil, swt, zt, a2Ts,
                 lgr, weT, gwT, sel8] + u1g + u3g + a1T + a3T)
    FFN_A_BUFS = [w1Ts, w3Ts, b1Ts, b3Ts]
    FFN_B_BUFS = [w2Ts, b2Ts]
    ATTN_BUFS = [qw, kw, vw, was, wvs, vstage, pP, rsb, att, gstage,
                 gstage2, rsiA, rsiB]

    bank = [psum(f"bk{i}", (128, 512)) for i in range(8)]

    def bkT(p0, p1, f0, f1):
        return bank[7].h.bitcast(BF16)[p0:p1, f0:f1]

    bkTbuf = bank[7]

    sems = {k: ctx.enter_context(nc.semaphore(k))
            for k in ["pe", "dve", "act", "pool", "dS", "dP", "dA", "cc"]}
    block = ctx.enter_context(nc.Block())
    p = Prog(nc)

    def phase_barrier(olds, news, tag):
        p.op("dve", lambda e: e.memset(arena.h[0:1, 0:1], 0.0),
             reads=olds, writes=list(olds) + list(news) + [arena], tag=tag)

    def dma(q, out, in_, reads=(), writes=(), tag=""):
        p.op(q, lambda e, o=out, i=in_: e.dma_start(out=o, in_=i),
             reads=reads, writes=writes, dma=True, tag=tag)

    def dmat(out, in_, reads=(), writes=(), tag=""):
        p.op("act", lambda e, o=out, i=in_: e.dma_start_transpose(o, i),
             reads=reads, writes=writes, dma=True, tag=tag)

    def mm(out, lhsT, rhs, start, stop, reads=(), writes=(), tag=""):
        p.op("pe", lambda e, o=out, l=lhsT, r=rhs, s1=start, s2=stop:
             e.matmul(o, l, r, start=s1, stop=s2),
             reads=reads, writes=writes, tag=tag)

    def tr(out, in_, ident, start, reads=(), writes=(), tag=""):
        if start:
            p.op("pe", lambda e, o=out, i=in_, d=ident: e.transpose(o, i, d),
                 reads=reads, writes=writes, tag=tag)
        else:
            p.op("pe", lambda e, o=out, i=in_, d=ident:
                 e.matmul(o, i, d, is_transpose=True, start=False, stop=True),
                 reads=reads, writes=writes, tag=tag)

    def cp(eng, out, in_, reads, writes, tag=""):
        p.op(eng, lambda e, o=out, i=in_: e.tensor_copy(out=o, in_=i),
             reads=reads, writes=writes, tag=tag)

    def tt(eng, out, in0, in1, op, reads, writes, tag=""):
        p.op(eng, lambda e, o=out, a=in0, b=in1, c=op:
             e.tensor_tensor(out=o, in0=a, in1=b, op=c),
             reads=reads, writes=writes, tag=tag)

    def ts1(eng, out, in0, s1, op0, reads, writes, tag=""):
        p.op(eng, lambda e, o=out, a=in0, x=s1, c0=op0:
             e.tensor_scalar(out=o, in0=a, scalar1=x, scalar2=None, op0=c0),
             reads=reads, writes=writes, tag=tag)

    def stt(eng, out, in0, sc, in1, op0, op1, reads, writes, tag=""):
        p.op(eng, lambda e, o=out, a=in0, s=sc, b=in1, c0=op0, c1=op1:
             e.scalar_tensor_tensor(out=o, in0=a, scalar=s, in1=b,
                                    op0=c0, op1=c1),
             reads=reads, writes=writes, tag=tag)

    def act(out, in_, func, bias=0.0, scale=1.0, reads=(), writes=(), tag=""):
        p.op("act", lambda e, o=out, i=in_, f=func, b=bias, s=scale:
             e.activation(out=o, in_=i, func=f, bias=b, scale=s),
             reads=reads, writes=writes, tag=tag)

    def recip(out, in_, reads, writes, tag=""):
        p.op("dve", lambda e, o=out, i=in_: e.reciprocal(out=o, in_=i),
             reads=reads, writes=writes, tag=tag)

    def rmax(out, in_, reads, writes, tag=""):
        p.op("dve", lambda e, o=out, i=in_: e.reduce_max(o, i, axis=AX.X),
             reads=reads, writes=writes, tag=tag)

    def memset(buf, val, apx=None):
        a = apx if apx is not None else buf[:]
        p.op("dve", lambda e, x=a, v=val: e.memset(x, v), writes=[buf])

    def r32(apx):
        return apx.bitcast(F32R)

    tapd = {}
    for name, shape, dt_ in [("ln1", (D, TPC), BF16), ("we", (TPC, E), F32),
                             ("u1", (3 * 96, 512), BF16),
                             ("zs0", (FF, 512), BF16),
                             ("vp0", (16 * E, 512), BF16),
                             ("x1", (D, TPC), F32R), ("qf", (D, TPC), BF16),
                             ("kf", (D, TPC), BF16), ("vt", (TPC, HVP), BF16),
                             ("at0", (D, 512), BF16), ("x2", (D, TPC), F32R),
                             ("gwt", (128, DC * E), F32),
                             ("a1t", (128, DC * 96), BF16),
                             ("lgr0", (8, 512), F32),
                             ("ltk0", (128, 4 * E), F32),
                             ("m1x", (128, 1), F32), ("m2x", (128, 1), F32),
                             ("dvx", (128, 1), F32), ("gws0", (8, 2048), F32),
                             ("gstg", (128, 128), F32),
                             ("gwtd", (128, 128), F32),
                             ("a2t0", (128, FC * R), BF16),
                             ("b2t0", (16, D), BF16),
                             ("b1t0", (16, FF), BF16)]:
        if name in taps:
            tapd[name] = dram("tp_" + name, shape, dt_, "ExternalOutput")

    dma("sp", eye[:], eye_d.h[:, :], writes=[eye])
    dma("sp", eyer[:], eyer_d.h[:, :], writes=[eyer])
    dma("pool", eye16[:], eye_d.h[:, :], writes=[eye16])
    dma("pool", eye16x2[:], eye2_d.h[:, :], writes=[eye16x2])
    dma("pool", masks[:], mask_d.h.rearrange("m p n -> p m n"), writes=[masks])
    dma("sp", biask[:], bik_d.h[:, :], writes=[biask])
    dma("pool", aq[:], aq_d.h[:, :, :], writes=[aq])
    dma("sp", indA[:], inda_d.h[:, :], writes=[indA])
    dma("sp", indB[:], indb_d.h[:, :], writes=[indB])
    dma("sp", gidx[:], gix_d.h[:, :], writes=[gidx])
    dma("sp", bav[:], batT.h[:, :], writes=[bav])
    dma("pool", vbias[:],
        bat.h.rearrange("(g r n) p -> g r (n p)", g=3, n=4)[2:3, :, :],
        writes=[vbias])
    dma("sp", bpv[:], bprT.h[:, :], writes=[bpv])
    memset(epsb, EPS)
    ts1("dve", bavq[:], bav[:, 0:16], 0.125, ALU.mult,
        reads=[bav], writes=[bavq])
    memset(ones1, 1.0)
    memset(ones1h, 1.0)
    memset(onesc, 1.0)

    dma("sp", gws[:], gw_d.h[:, :], writes=[gws])
    for c4 in range(4):
        for j in range(4):
            dc = 4 * c4 + j
            tr(bank[0][:, 8 * j:8 * j + 8], gws[0:E, 128 * dc:128 * dc + 128],
               eye[0:8, 0:8], start=(j == 0), reads=[gws, eye], writes=[bank[0]])
        cp("dve", gwstg[:, 32 * c4:32 * c4 + 32], bank[0][:, 0:32],
           reads=[bank[0]], writes=[gwstg])

    if "gws0" in tapd:
        dma("sp", tapd["gws0"].h[:, :], gws[:, :], reads=[gws])
    if "gstg" in tapd:
        dma("sp", tapd["gstg"].h[:, :], gwstg[:, :], reads=[gwstg])
    dma("sp", gwTd.h[:, :], gwstg[:, :], reads=[gwstg], writes=[gwTd], tag="gw_st")
    if "gwtd" in tapd:
        dma("sp", tapd["gwtd"].h[:, :], gwTd.h[:, :], reads=[gwstg])
    for wsrc, wdst, colsrc in ((w1_d, w1T, False), (w3_d, w3T, False),
                               (w2_d, w2T, True)):
        for fc in range(FC):
            sl = fc % 2
            if colsrc:
                src = wsrc.h.rearrange("(c p) f -> p c f", p=128)[
                    :, :, 128 * fc:128 * fc + 128]
                dst = wnat.h[:, sl, :].rearrange("p (c f) -> p c f", f=128)
            else:
                src = wsrc.h[128 * fc:128 * fc + 128, :]
                dst = wnat.h[:, sl, :]
            dma("pool", dst, src, reads=[wnat], writes=[wnat], tag="wc_ld")
            for half in range(2):
                for j in range(8):
                    dc = 8 * half + j
                    tr(bkT(0, 128, 128 * j, 128 * j + 128),
                       wnat[:, sl, 128 * dc:128 * dc + 128], eye16[:],
                       start=(j == 0), reads=[wnat, eye16], writes=[bkTbuf],
                       tag="wc_tr")
                cp("dve", wtt[:, sl, 8 * half:8 * half + 8, :],
                   bkT(0, 128, 0, 1024), reads=[bkTbuf], writes=[wtt],
                   tag="wc_ev")
            dma("sp", wdst.h[fc].rearrange("dc p f -> p dc f"), wtt[:, sl, :, :],
                reads=[wtt], writes=[wdst], tag="wc_st")

    EPERM = [0, 3, 6, 1, 4, 7, 2, 5]
    for bsrc, bdst in ((b1_d, b1T), (b3_d, b3T)):
        for pos in range(E):
            e_ = EPERM[pos]
            dma("pool", bnat[:], bsrc.h[e_].rearrange("(c p) r -> p c r", p=128),
                reads=[bnat], writes=[bnat], tag="bT_ld")
            for q4 in range(11):
                for j in range(4):
                    fc = 4 * q4 + j
                    tr(bkT(0, 16, 128 * j, 128 * j + 128), bnat[:, fc, :],
                       eye16[:], start=(j == 0),
                       reads=[bnat, eye16], writes=[bkTbuf])
                cp("dve", tev[0:16, 512 * q4:512 * q4 + 512], bkT(0, 16, 0, 512),
                   reads=[bkTbuf], writes=[tev])
            dma("sp", bdst.h[:, pos, :], tev[0:16, :], reads=[tev], writes=[bdst], tag="bT_st")

    for e_ in range(E):
        dma("pool", a2nat[:], a2_d.h[e_, :, :], reads=[a2nat], writes=[a2nat],
            tag="a2_ld")
        for q8 in range(6):
            nj = 8 if q8 < 5 else 4
            for j in range(nj):
                fc = 8 * q8 + j
                tr(bkT(0, 128, R * j, R * j + R),
                   a2nat[0:16, 128 * fc:128 * fc + 128], eye16[0:16, 0:16],
                   start=(j == 0), reads=[a2nat, eye16], writes=[bkTbuf])
            cp("dve", tev2[:, R * 8 * q8:R * (8 * q8 + nj)],
               bkT(0, 128, 0, R * nj), reads=[bkTbuf], writes=[tev2])
        dma("sp", a2Td.h[e_, :, :], tev2[:, 0:FC * R], reads=[tev2], writes=[a2Td], tag="a2_st")

    for e_ in range(E):
        dma("pool", b2nat[:], b2_d.h[e_].rearrange("(c p) r -> p c r", p=128),
            reads=[b2nat], writes=[b2nat], tag="b2_ld")
        for q4 in range(4):
            for j in range(4):
                dc = 4 * q4 + j
                tr(bkT(0, 16, 128 * j, 128 * j + 128), b2nat[:, dc, :],
                   eye16[:], start=(j == 0), reads=[b2nat, eye16],
                   writes=[bkTbuf])
            cp("dve", tev3[0:16, 512 * q4:512 * q4 + 512], bkT(0, 16, 0, 512),
               reads=[bkTbuf], writes=[tev3])
        dma("sp", b2Td.h[:, e_, :], tev3[0:16, :], reads=[tev3], writes=[b2Td], tag="b2_st")

    if "a2t0" in tapd:
        dma("sp", tapd["a2t0"].h[:, :], a2Td.h[0, :, :], reads=[tev2])
    if "b2t0" in tapd:
        dma("sp", tapd["b2t0"].h[:, :], b2Td.h[:, 0, :], reads=[tev3])
    if "b1t0" in tapd:
        dma("sp", tapd["b1t0"].h[:, :], b1T.h[:, 0, :], reads=[tev])
    for asrc, adst in ((a1_d, a1Td), (a3_d, a3Td)):
        for g in range(3):
            gsz = 3 if g < 2 else 2
            memset(apad, 0.0)
            for gi in range(gsz):
                dma("sp", apad.h[32 * gi:32 * gi + 16, :],
                    asrc.h[3 * g + gi, :, :], reads=[apad], writes=[apad],
                    tag="a_ld")
            for q4 in range(4):
                for j in range(4):
                    dc = 4 * q4 + j
                    tr(bkT(0, 96, 128 * j, 128 * j + 96)
                       if False else bank[0][:, 96 * j:96 * j + 96],
                       apad[0:96, 128 * dc:128 * dc + 128], eye[0:96, 0:96],
                       start=(j == 0), reads=[apad, eye], writes=[bank[0]])
                cp("dve", tev2[:, 96 * 4 * q4:96 * (4 * q4 + 4)],
                   bank[0][:, 0:384], reads=[bank[0]], writes=[tev2])
            dma("sp", adst.h[g, :, :], tev2[:, 0:DC * 96], reads=[tev2],
                writes=[adst], tag="aT_st")

    def layernorm(xsrc, tt_i, with_gate):
        p.phase = "ln"
        c0 = 512 * tt_i
        for dc in range(DC):
            sl = dc % 2
            dma("pool", xch[:, sl, :], xsrc.h[128 * dc:128 * dc + 128, c0:c0 + 512],
                reads=[xch, xsrc], writes=[xch], tag="ln_ldx")
            mm(bank[0][0:1, :], r32(onesc[:]), xch[:, sl, :],
               start=(dc == 0), stop=(dc == DC - 1),
               reads=[onesc, xch], writes=[bank[0]], tag="ln_sum")
            act(sqs[:, 0, :], xch[:, sl, :], AF.Square,
                reads=[xch], writes=[sqs], tag="ln_sq")
            mm(bank[1][0:1, :], r32(onesc[:]), sqs[:, 0, :],
               start=(dc == 0), stop=(dc == DC - 1),
               reads=[onesc, sqs], writes=[bank[1]], tag="ln_sqsum")
        ts1("dve", rowA[:], bank[0][0:1, :], 1.0 / D, ALU.mult,
            reads=[bank[0]], writes=[rowA])
        tt("dve", rowB[:], rowA[:], rowA[:], ALU.mult, reads=[rowA],
           writes=[rowB])
        stt("dve", rowB[:], bank[1][0:1, :], 1.0 / D, rowB[:], ALU.mult,
            ALU.subtract, reads=[bank[1], rowB], writes=[rowB])
        act(rowB[:], rowB[:], AF.Sqrt, bias=epsb.h[:, :], reads=[rowB, epsb], writes=[rowB])
        mm(bank[2][:, :], ones1[:], rowA[:], start=True, stop=True,
           reads=[ones1, rowA], writes=[bank[2]])
        cp("dve", meanB[:], bank[2][:, :], reads=[bank[2]], writes=[meanB])
        recip(rowB[:], rowB[:], reads=[rowB], writes=[rowB])
        mm(bank[2][:, :], ones1[:], rowB[:], start=True, stop=True,
           reads=[ones1, rowB], writes=[bank[2]])
        cp("dve", rstdB[:], bank[2][:, :], reads=[bank[2]], writes=[rstdB])
        for dc in range(DC):
            sl = dc % 2
            dma("pool", xch[:, sl, :], xsrc.h[128 * dc:128 * dc + 128, c0:c0 + 512],
                reads=[xch, xsrc], writes=[xch], tag="ln_ldx2")
            tt("dve", t1s[:, sl, :], xch[:, sl, :].bitcast(F32), meanB[:], ALU.subtract,
               reads=[xch, meanB], writes=[t1s], tag="ln_t1")
            tt("dve", xn16[:, dc, :], t1s[:, sl, :], rstdB[:], ALU.mult,
               reads=[t1s, rstdB], writes=[xn16], tag="ln_app")
            if with_gate:
                cp("dve", t1hi[:, sl, :], t1s[:, sl, :], reads=[t1s],
                   writes=[t1hi], tag="t1hi")
                tt("dve", t1lo[:, sl, :], t1s[:, sl, :], t1hi[:, sl, :],
                   ALU.subtract, reads=[t1s, t1hi], writes=[t1lo], tag="t1lo")
                mm(bank[3][0:E, :], gwhi[:, dc, :], t1hi[:, sl, :],
                   start=(dc == 0), stop=False,
                   reads=[gwhi, t1hi], writes=[bank[3]], tag="gate_hh")
                mm(bank[3][0:E, :], gwlo[:, dc, :], t1hi[:, sl, :],
                   start=False, stop=False,
                   reads=[gwlo, t1hi], writes=[bank[3]], tag="gate_lh")
                mm(bank[3][0:E, :], gwhi[:, dc, :], t1lo[:, sl, :],
                   start=False, stop=(dc == DC - 1),
                   reads=[gwhi, t1lo], writes=[bank[3]], tag="gate_hl")
        if with_gate:
            tt("dve", lgr[:], bank[3][0:E, :], rstdB[0:E, :], ALU.mult,
               reads=[bank[3], rstdB], writes=[lgr], tag="gate_sc")

    def top2(tt_i):
        p.phase = "top2"
        if tap_en[0] and tt_i == 0 and "lgr0" in tapd:
            dma("sp", tapd["lgr0"].h[:, :], lgr[0:8, :], reads=[lgr])
        for j in range(4):
            tr(bank[4][:, 8 * j:8 * j + 8], lgr[:, 128 * j:128 * j + 128],
               eye[0:8, 0:8], start=(j == 0), reads=[lgr, eye],
               writes=[bank[4]])
        cp("dve", ltok[:], bank[4][:, 0:32], reads=[bank[4]], writes=[ltok])
        for b_ in range(4):
            lt = ltok[:, b_, :]
            rmax(m1t[:], lt, reads=[ltok], writes=[m1t])
            ts1("dve", m1n[:], m1t[:], -1.0, ALU.mult, reads=[m1t],
                writes=[m1n])
            ts1("dve", mskt[:], lt, m1t[:], ALU.is_equal, reads=[ltok, m1t],
                writes=[mskt])
            ts1("dve", wtmp[:], mskt[:], -1e30, ALU.mult, reads=[mskt],
                writes=[wtmp])
            tt("dve", wtmp[:], wtmp[:], lt, ALU.add, reads=[wtmp, ltok],
               writes=[wtmp])
            rmax(m2t[:], wtmp[:], reads=[wtmp], writes=[m2t])
            p.op("dve", lambda e: e.reduce_sum(cntt[:], mskt[:], axis=AX.X),
                 reads=[mskt], writes=[cntt])
            ts1("dve", cntt[:], cntt[:], 1.5, ALU.is_ge, reads=[cntt],
                writes=[cntt])
            tt("dve", dif1[:], m1t[:], m2t[:], ALU.subtract,
               reads=[m1t, m2t], writes=[dif1])
            tt("dve", dif1[:], dif1[:], cntt[:], ALU.mult,
               reads=[dif1, cntt], writes=[dif1])
            tt("dve", m2t[:], m2t[:], dif1[:], ALU.add, reads=[m2t, dif1],
               writes=[m2t])
            act(expt[:], lt, AF.Exp, bias=m1n[:], reads=[ltok, m1n],
                writes=[expt])
            act(exp2t[:], m2t[:], AF.Exp, bias=m1n[:], reads=[m2t, m1n],
                writes=[exp2t])
            ts1("dve", exp2t[:], exp2t[:], 1.0, ALU.add, reads=[exp2t],
                writes=[exp2t])
            recip(dinv[:], exp2t[:], reads=[exp2t], writes=[dinv])
            ts1("dve", mskt[:], lt, m2t[:], ALU.is_ge, reads=[ltok, m2t],
                writes=[mskt])
            tt("dve", wtmp[:], expt[:], mskt[:], ALU.mult,
               reads=[expt, mskt], writes=[wtmp])
            ts1("dve", wetok[:, b_, :], wtmp[:], dinv[:], ALU.mult,
                reads=[wtmp, dinv], writes=[wetok])
        if tap_en[0] and tt_i == 0 and "ltk0" in tapd:
            dma("sp", tapd["ltk0"].h[:, :],
                ltok.h[:, :, :].rearrange("p a b -> p (a b)"), reads=[ltok])
        if tap_en[0] and tt_i == 0 and "m1x" in tapd:
            dma("sp", tapd["m1x"].h[:, :], m1t[:], reads=[m1t])
            dma("sp", tapd["m2x"].h[:, :], m2t[:], reads=[m2t])
            dma("sp", tapd["dvx"].h[:, :], dinv[:], reads=[dinv])
        if tap_en[0] and "we" in tapd:
            dma("sp", tapd["we"].h[512 * tt_i:512 * tt_i + 512, :].rearrange(
                "(b p) e -> p b e", p=128), wetok[:], reads=[wetok])
        for j in range(4):
            tr(bank[4][0:E, 128 * j:128 * j + 128], wetok[:, j, :], eye[:],
               start=(j == 0), reads=[wetok, eye], writes=[bank[4]])
        cp("dve", weT[:], bank[4][0:E, :], reads=[bank[4]], writes=[weT])
        for e_ in range(E):
            mm(bank[4][:, :], sel8[:, 128 * e_:128 * e_ + 128],
               weT[0:8, :], start=True, stop=True,
               reads=[sel8, weT], writes=[bank[4]])
            cp("dve", weB[:, e_, :], bank[4][:, :], reads=[bank[4]],
               writes=[weB])

    def umm(tt_i):
        p.phase = "umm"
        for pre, adst, asrcd, ug in ((1, a1T, a1Td, u1g), (3, a3T, a3Td, u3g)):
            for g in range(3):
                if tt_i == 0:
                    dma("sp", adst[g][:, :, :],
                        asrcd.h[g].rearrange("p (dc n) -> p dc n", n=96),
                        reads=[adst[g], asrcd], writes=[adst[g]], tag="aT_ld")
                for dc in range(DC):
                    mm(bank[5][0:96, :], adst[g][:, dc, :], xn16[:, dc, :],
                       start=(dc == 0), stop=(dc == DC - 1),
                       reads=[adst[g], xn16], writes=[bank[5]], tag="umm")
                ts1("dve", ug[g][:, :], bank[5][0:96, :], 2.0, ALU.mult,
                    reads=[bank[5]], writes=[ug[g]], tag="uev")
        if tap_en[0] and tt_i == 0 and "a1t" in tapd:
            dma("sp", tapd["a1t"].h[:, :],
                a1T[0].h[:, :, :].rearrange("p a b -> p (a b)"), reads=[a1T[0]])
        if tap_en[0] and tt_i == 0 and "u1" in tapd:
            for g in range(3):
                dma("sp", tapd["u1"].h[96 * g:96 * g + 96, :], u1g[g][:, :],
                    reads=[u1g[g]])

    def ffn_phase_a(tt_i):
        p.phase = "ffa"
        phase_barrier(FFN_B_BUFS, FFN_A_BUFS, "ffb->ffa")
        for fc in range(FC):
            sl = fc % 2
            dma("act", w1Ts[:, sl, :, :], w1T.h[fc].rearrange("dc p f -> p dc f"),
                reads=[w1Ts, w1T], writes=[w1Ts], tag="w1T_ld")
            dma("sp", w3Ts[:, sl, :, :], w3T.h[fc].rearrange("dc p f -> p dc f"),
                reads=[w3Ts, w3T], writes=[w3Ts], tag="w3T_ld")
            for jb in range(3):
                ne = 3 if jb < 2 else 2
                dma("act", b1Ts[32 * jb:32 * jb + 16, sl, 3 * jb:3 * jb + ne, :],
                    b1T.h[:, 3 * jb:3 * jb + ne, 128 * fc:128 * fc + 128],
                    reads=[b1Ts, b1T], writes=[b1Ts], tag="b1T_ld")
                dma("sp", b3Ts[32 * jb:32 * jb + 16, sl, 3 * jb:3 * jb + ne, :],
                    b3T.h[:, 3 * jb:3 * jb + ne, 128 * fc:128 * fc + 128],
                    reads=[b3Ts, b3T], writes=[b3Ts], tag="b3T_ld")
            for dc in range(DC):
                mm(bank[0][:, :], w1Ts[:, sl, dc, :], xn16[:, dc, :],
                   start=(dc == 0), stop=(dc == DC - 1),
                   reads=[w1Ts, xn16], writes=[bank[0]], tag="base1")
            cp("dve", base1s[:], bank[0][:, :], reads=[bank[0]],
               writes=[base1s], tag="b1ev")
            for dc in range(DC):
                mm(bank[1][:, :], w3Ts[:, sl, dc, :], xn16[:, dc, :],
                   start=(dc == 0), stop=(dc == DC - 1),
                   reads=[w3Ts, xn16], writes=[bank[1]], tag="base3")
            cp("dve", base3s[:], bank[1][:, :], reads=[bank[1]],
               writes=[base3s], tag="b3ev")
            for e_ in range(E):
                g, j = e_ // 3, e_ % 3
                esl = e_ % 2
                h1b = bank[0] if esl == 0 else bank[2]
                h3b = bank[1] if esl == 0 else bank[7]
                mm(h1b[:, :], eyer[:], base1s[:], start=True,
                   stop=False, reads=[eyer, base1s], writes=[h1b],
                   tag="inj1")
                mm(h1b[:, :], b1Ts[:, sl, 3 * j + g, :],
                   u1g[g][0:96, :], start=False, stop=True,
                   reads=[b1Ts, u1g[g]], writes=[h1b], tag="d1")
                mm(h3b[:, :], eyer[:], base3s[:], start=True,
                   stop=False, reads=[eyer, base3s], writes=[h3b],
                   tag="inj3")
                mm(h3b[:, :], b3Ts[:, sl, 3 * j + g, :],
                   u3g[g][0:96, :], start=False, stop=True,
                   reads=[b3Ts, u3g[g]], writes=[h3b], tag="d3")
                act(ssil[:, esl, :], h1b[:, :], AF.Silu,
                    reads=[h1b], writes=[ssil], tag="silu")
                tt("dve", swt[:, esl, :], ssil[:, esl, :], weB[:, e_, :],
                   ALU.mult, reads=[ssil, weB], writes=[swt], tag="sw")
                tt("dve", zt[:, esl, :], swt[:, esl, :], h3b[:, :],
                   ALU.mult, reads=[swt, h3b], writes=[zt], tag="z")
                if e_ == 0:
                    cp("pool", accf[:], zt[:, esl, :], reads=[zt],
                       writes=[accf], tag="acc0")
                else:
                    tt("pool", accf[:], accf[:], zt[:, esl, :], ALU.add,
                       reads=[accf, zt], writes=[accf], tag="acc")
                if e_ > 0:
                    ep = e_ - 1
                    vb = bank[3 + ep // 2]
                    v0 = 32 * (ep % 2)
                    mm(vb[v0:v0 + 16, :], a2Ts[:, ep, R * fc:R * fc + R],
                       zt[:, ep % 2, :], start=(fc == 0), stop=(fc == FC - 1),
                       reads=[a2Ts, zt], writes=[vb], tag="vmm")
            mm(bank[6][32:48, :], a2Ts[:, 7, R * fc:R * fc + R],
               zt[:, 1, :], start=(fc == 0), stop=(fc == FC - 1),
               reads=[a2Ts, zt], writes=[bank[6]], tag="vmm")
            cp("dve", zsum[:, fc, :], accf[:], reads=[accf], writes=[zsum],
               tag="zsumev")
        for e_ in range(E):
            vb = bank[3 + e_ // 2]
            v0 = 32 * (e_ % 2)
            ts1("dve", vpw[0:16, e_, :], vb[v0:v0 + 16, :], 2.0, ALU.mult,
                reads=[vb], writes=[vpw], tag="vpw")
        if tap_en[0] and tt_i == 0 and "zs0" in tapd:
            dma("sp", tapd["zs0"].h.rearrange("(c p) t -> p c t", p=128),
                zsum[:], reads=[zsum])
        if tap_en[0] and tt_i == 0 and "vp0" in tapd:
            dma("sp", tapd["vp0"].h.rearrange("(c p) t -> p c t", p=16),
                vpw[:], reads=[vpw])

    def ffn_phase_b(tt_i, xsrc, xdst, tapx):
        p.phase = "ffb"
        phase_barrier(FFN_A_BUFS, FFN_B_BUFS, "ffa->ffb")
        c0 = 512 * tt_i
        for dh in range(2):
            for fc in range(FC):
                sl = fc % 2
                dma("act", w2Ts[:, sl, :, :],
                    w2T.h[fc, 8 * dh:8 * dh + 8].rearrange("dc p f -> p dc f"),
                    reads=[w2Ts, w2T], writes=[w2Ts], tag="w2T_ld")
                for dj in range(8):
                    mm(bank[dj][:, :], w2Ts[:, sl, dj, :], zsum[:, fc, :],
                       start=(fc == 0), stop=False,
                       reads=[w2Ts, zsum], writes=[bank[dj]], tag="outmm")
            for dq in range(2):
                dma("sp", b2Ts[:],
                    b2Td.h[:, :, 1024 * dh + 512 * dq:1024 * dh + 512 * dq + 512],
                    reads=[b2Ts, b2Td], writes=[b2Ts], tag="b2T_ld")
                for e_ in range(E):
                    for dj4 in range(4):
                        dj = 4 * dq + dj4
                        mm(bank[dj][:, :],
                           b2Ts[0:16, e_, 128 * dj4:128 * dj4 + 128],
                           vpw[0:16, e_, :], start=False,
                           stop=(e_ == E - 1), reads=[b2Ts, vpw],
                           writes=[bank[dj]], tag="b2mm")
            for dj in range(8):
                dc = 8 * dh + dj
                sl = dj % 2
                dma("pool", xch[:, sl, :],
                    xsrc.h[128 * dc:128 * dc + 128, c0:c0 + 512],
                    reads=[xch, xsrc], writes=[xch], tag="fb_ldx")
                stt("dve", ost[:, sl, :], bank[dj][:, :], 0.5,
                    xch[:, sl, :].bitcast(F32), ALU.mult, ALU.add, reads=[bank[dj], xch], writes=[ost],
                    tag="fb_res")
                dma("sp", xdst.h[128 * dc:128 * dc + 128, c0:c0 + 512],
                    ost[:, sl, :], reads=[ost], writes=[xdst], tag="fb_stx")
        if tapx is not None and tapx in tapd:
            for dc in range(DC):
                dma("sp", tapd[tapx].h[128 * dc:128 * dc + 128, c0:c0 + 512],
                    xdst.h[128 * dc:128 * dc + 128, c0:c0 + 512],
                    reads=[xdst])

    def qkv(tt_i):
        p.phase = "qkv"
        c0 = 512 * tt_i
        for fc in range(32):
            sl = fc % 2
            dma("pool", was[:, sl, :, :],
                Wat.h.rearrange("(c p) f -> p c f", p=128)[
                    :, :, 128 * fc:128 * fc + 128],
                reads=[was], writes=[was], tag="wa_ld")
            bk = bank[fc % 3]
            for dc in range(DC):
                mm(bk[:, :], was[:, sl, dc, :], xn16[:, dc, :],
                   start=(dc == 0), stop=(dc == DC - 1),
                   reads=[was, xn16], writes=[bk], tag="qk_mm")
            if fc < 16:
                p.op("dve", lambda e, o=qkst[:, sl, :], i=bk[:, :],
                     a=bavq[:, fc:fc + 1]: e.tensor_scalar(
                         out=o, in0=i, scalar1=0.125, scalar2=a,
                         op0=ALU.mult, op1=ALU.add),
                     reads=[bk, bavq], writes=[qkst], tag="q_ev")
            else:
                p.op("act", lambda e, o=qkst[:, sl, :], i=bk[:, :],
                     a=bav[:, fc:fc + 1]: e.add(out=o, in_=i, add=a),
                     reads=[bk, bav], writes=[qkst], tag="k_ev")
            dstd = qfd if fc < 16 else kfd
            fcl = fc % 16
            dma("sp", dstd.h[128 * fcl:128 * fcl + 128, c0:c0 + 512],
                qkst[:, sl, :], reads=[qkst], writes=[dstd], tag="qk_st")
            if fc >= 16 and tt_i == NTT - 1:
                dma("sp", kbin.h[128 * fcl:128 * fcl + 128, :],
                    qkst[:, sl, 128:512], reads=[qkst], writes=[kbin],
                    tag="kb_st")
        for tb in range(4):
            tbg = 4 * tt_i + tb
            for vs in range(4):
                dma("pool", wvs[:, :, :],
                    Wat.h.rearrange("(c p) f -> p c f", p=128)[
                        :, :, 4096 + 512 * vs:4096 + 512 * vs + 512],
                    reads=[wvs], writes=[wvs], tag="wv_ld")
                bk = bank[3 + vs % 2]
                for dc in range(DC):
                    mm(bk[:, :], xn16[:, dc, 128 * tb:128 * tb + 128],
                       wvs[:, dc, :], start=(dc == 0), stop=False,
                       reads=[wvs, xn16], writes=[bk], tag="v_mm")
                mm(bk[:, :], r32(ones1[:]).bitcast(F32R) if False else
                   ones1h[:], vbias[0:1, vs, :], start=False, stop=True,
                   reads=[ones1h, vbias], writes=[bk], tag="v_bias")
                cp("dve",
                   vstage.h.rearrange("p (h v) -> p h v", v=VP)[
                       :, 8 * vs:8 * vs + 8, 0:64],
                   bk.h[:, :].rearrange("p (h v) -> p h v", v=64),
                   reads=[bk], writes=[vstage], tag="v_ev")
            dma("sp", vtd.h[128 * tbg:128 * tbg + 128, :], vstage[:],
                reads=[vstage], writes=[vtd], tag="v_st")
            if tbg >= NTB - 3:
                dma("sp", vbin.h[128 * (tbg - (NTB - 3)):
                                 128 * (tbg - (NTB - 3)) + 128, :],
                    vstage[:], reads=[vstage], writes=[vbin], tag="vb_st")

    def boundary():
        p.phase = "bnd"
        phase_barrier([vstage, pP, rsb, att], [gstage, gstage2], "qkv->bnd")
        from concourse.bass_core_rust import IndirectOffsetOnAxis as IOA
        p.op("pool", lambda e: e.collective_compute(
            "AllGather", ALU.bypass, [list(range(8))],
            [kbin.h.rearrange("(a r) t -> a (r t)", a=16).opt()],
            [kbg.h[:, :].opt()]), reads=[kbin], writes=[kbg], cc=True)
        p.op("pool", lambda e: e.collective_compute(
            "AllGather", ALU.bypass, [list(range(8))],
            [vbin.h.rearrange("(a r) t -> a (r t)", a=16).opt()],
            [vbg.h[:, :].opt()]), reads=[vbin], writes=[vbg], cc=True)
        p.op("pool", lambda e: e.indirect_dma_start(
            gstage[:, :], None, kbg.h.rearrange("a (b t) -> (a b) t", b=8),
            IOA(gidx[:], 0)), reads=[kbg, gidx, gstage], writes=[gstage],
            dma=True, tag="kb_g")
        dma("sp", kbp.h.rearrange("(a i) t -> a (i t)", a=128),
            gstage[:, :], reads=[gstage], writes=[kbp], tag="kb_w")
        p.op("pool", lambda e: e.indirect_dma_start(
            gstage2[:, :], None, vbg.h.rearrange("a (b t) -> (a b) t", b=8),
            IOA(gidx[:], 0)), reads=[vbg, gidx, gstage2],
            writes=[gstage2], dma=True, tag="vb_g")
        dma("sp", vbp.h.rearrange("(a i) t -> a (i t)", a=128),
            gstage2[:, :], reads=[gstage2], writes=[vbp], tag="vb_w")

    def attention(qt, xsrc, xdst):
        p.phase = "att"
        c0 = 512 * qt
        dma("act", qw[:, :, :], qfd.h.rearrange("(c p) t -> p c t", p=128)[
            :, :, c0:c0 + 512], reads=[qw, qfd], writes=[qw], tag="qw_ld")
        for kc in range(NKC):
            kl0 = 512 * qt - WIN + 128 * kc
            if kl0 < 0:
                dma("act", kw[:, :, 128 * kc:128 * kc + 128],
                    kbp.h.rearrange("(c p) t -> p c t", p=128)[
                        :, :, WIN + kl0:WIN + kl0 + 128],
                    reads=[kw, kfd, kbp], writes=[kw], tag="kw_ld")
                dma("sp", vw[:, kc, :],
                    vbp.h.rearrange("(b p) c -> p b c", p=128)[
                        :, (WIN + kl0) // 128, :],
                    reads=[vw, vtd, vbp], writes=[vw], tag="vw_ld")
            else:
                dma("act", kw[:, :, 128 * kc:128 * kc + 128],
                    kfd.h.rearrange("(c p) t -> p c t", p=128)[
                        :, :, kl0:kl0 + 128],
                    reads=[kw, kfd, kbp], writes=[kw], tag="kw_ld")
                dma("sp", vw[:, kc, :],
                    vtd.h.rearrange("(b p) c -> p b c", p=128)[
                        :, kl0 // 128, :],
                    reads=[vw, vtd, vbp], writes=[vw], tag="vw_ld")
        for hp in range(16):
            for h2 in range(2):
                h_ = 2 * hp + h2
                r0 = 64 * h2
                pvb = bank[2 + h2]
                for kc in range(NKC):
                    sbk = bank[h2] if kc % 2 == 0 else bank[5 + h2]
                    psl = 2 * h2 + kc % 2
                    mm(sbk[:, :], kw[r0:r0 + 64, hp, 128 * kc:128 * kc + 128],
                       qw[r0:r0 + 64, hp, :], start=True, stop=False,
                       reads=[kw, qw], writes=[sbk], tag="qk")
                    mm(sbk[:, :], ones1h[:], aq[0:1, qt, :],
                       start=False, stop=True, reads=[ones1h, aq],
                       writes=[sbk], tag="alibi")
                    if kc >= 3:
                        tt("dve", sbk[:, :], sbk[:, :], masks[:, kc - 3, :],
                           ALU.add, reads=[sbk, masks], writes=[sbk],
                           tag="mask")
                    act(pP[:, psl, :], sbk[:, :], AF.Exp,
                        bias=biask[:, NKC * qt + kc:NKC * qt + kc + 1],
                        reads=[sbk, biask], writes=[pP], tag="exp")
                    mm(pvb[0:VP, :], vw[:, kc, VP * h_:VP * h_ + VP],
                       pP[:, psl, :], start=(kc == 0), stop=(kc == NKC - 1),
                       reads=[vw, pP], writes=[pvb], tag="pv")
                recip((rsiA if h2 == 0 else rsiB)[:], pvb[64:65, :],
                      reads=[pvb], writes=[rsiA if h2 == 0 else rsiB],
                      tag="rsinv")
            mm(bank[4][:, :], indA[:], rsiA[:], start=True,
               stop=False, reads=[indA, rsiA], writes=[bank[4]], tag="rsb1")
            mm(bank[4][:, :], indB[:], rsiB[:], start=False,
               stop=True, reads=[indB, rsiB], writes=[bank[4]], tag="rsb2")
            cp("dve", rsb[:], bank[4][:, :], reads=[bank[4]], writes=[rsb])
            tt("dve", att[0:64, hp, :], bank[2][0:64, :], rsb[0:64, :],
               ALU.mult, reads=[bank[2], rsb], writes=[att], tag="nrmA")
            tt("dve", att[64:128, hp, :], bank[3][0:64, :], rsb[64:128, :],
               ALU.mult, reads=[bank[3], rsb], writes=[att], tag="nrmB")
        if qt == 0 and "at0" in tapd:
            dma("sp", tapd["at0"].h.rearrange("(c p) t -> p c t", p=128),
                att[:], reads=[att])
        for dc in range(DC):
            sl = dc % 2
            dma("pool", was[:, sl, :, :],
                Wpr.h.rearrange("(c p) f -> p c f", p=128)[
                    :, :, 128 * dc:128 * dc + 128],
                reads=[was], writes=[was], tag="wp_ld")
            bk = bank[5 + dc % 2]
            for fc in range(DC):
                mm(bk[:, :], was[:, sl, fc, :], att[:, fc, :],
                   start=(fc == 0), stop=(fc == DC - 1),
                   reads=[was, att], writes=[bk], tag="pr_mm")
            dma("pool", xch[:, sl, :],
                xsrc.h[128 * dc:128 * dc + 128, c0:c0 + 512],
                reads=[xch, xsrc], writes=[xch], tag="pr_ldx")
            stt("dve", ost[:, sl, :], bk[:, :], bpv[:, dc:dc + 1],
                xch[:, sl, :].bitcast(F32), ALU.add, ALU.add,
                reads=[bk, bpv, xch], writes=[ost], tag="pr_res")
            dma("sp", xdst.h[128 * dc:128 * dc + 128, c0:c0 + 512],
                ost[:, sl, :], reads=[ost], writes=[xdst], tag="pr_stx")

    phase_barrier(CACHE_BUFS, FFN_CORE + FFN_A_BUFS + FFN_B_BUFS, "cache->ffn")

    tap_en = [True]

    def ffn_sublayer(xsrc, xdst, tapx):
        memset(b1Ts, 0.0)
        memset(b3Ts, 0.0)
        dma("sp", gwT.h[:, :, :],
            gwTd.h.rearrange("p (dc e) -> p dc e", e=E), reads=[gwT, gwTd],
            writes=[gwT], tag="gwT_ld")
        dma("sp", sel8[:], sel_d.h[:, :], reads=[sel8], writes=[sel8],
            tag="sel_ld")
        for e_ in range(E):
            dma("sp", a2Ts[:, e_, :], a2Td.h[e_, :, :],
                reads=[a2Ts, a2Td], writes=[a2Ts], tag="a2s_ld")
        cp("dve", gwhi.h[:, :, :], gwT.h[:, :, :], reads=[gwT], writes=[gwhi])
        tt("dve", gwlo.h[:, :, :], gwT.h[:, :, :], gwhi.h[:, :, :],
           ALU.subtract, reads=[gwT, gwhi], writes=[gwlo])
        if tap_en[0] and "gwt" in tapd:
            dma("sp", tapd["gwt"].h[:, :],
                gwT.h[:, :, :].rearrange("p a b -> p (a b)"), reads=[gwT])
        for tt_i in range(NTT):
            layernorm(xsrc, tt_i, with_gate=True)
            if tt_i == 0 and xsrc is xin and "ln1" in tapd:
                dma("sp", tapd["ln1"].h.rearrange("(c p) t -> p c t", p=128)[
                    :, :, 0:512], xn16[:], reads=[xn16])
            top2(tt_i)
            umm(tt_i)
            ffn_phase_a(tt_i)
            ffn_phase_b(tt_i, xsrc, xdst, tapx)

    ffn_sublayer(xin, x1d, "x1")

    phase_barrier(FFN_CORE + FFN_A_BUFS + FFN_B_BUFS, ATTN_BUFS, "ffn->attn")
    memset(vstage, 0.0)
    memset(vstage, 1.0,
           vstage.h.rearrange("p (h v) -> p h v", v=VP)[:, :, 64:65])
    for tt_i in range(NTT):
        layernorm(x1d, tt_i, with_gate=False)
        qkv(tt_i)
    if "qf" in tapd:
        for dc in range(DC):
            dma("sp", tapd["qf"].h[128 * dc:128 * dc + 128, :],
                qfd.h[128 * dc:128 * dc + 128, :], reads=[qfd])
    if "kf" in tapd:
        for dc in range(DC):
            dma("sp", tapd["kf"].h[128 * dc:128 * dc + 128, :],
                kfd.h[128 * dc:128 * dc + 128, :], reads=[kfd])
    if "vt" in tapd:
        for tb in range(NTB):
            dma("sp", tapd["vt"].h[128 * tb:128 * tb + 128, :],
                vtd.h[128 * tb:128 * tb + 128, :], reads=[vtd])
    boundary()
    phase_barrier([gstage, gstage2], [pP, rsb, att], "bnd->attc")
    for qt in range(NTT):
        attention(qt, x1d, x2d)
    if "x2" in tapd:
        for dc in range(DC):
            dma("sp", tapd["x2"].h[128 * dc:128 * dc + 128, :],
                x2d.h[128 * dc:128 * dc + 128, :], reads=[x2d])

    phase_barrier(ATTN_BUFS, FFN_CORE + FFN_A_BUFS + FFN_B_BUFS, "attn->ffn")
    tap_en[0] = False
    ffn_sublayer(x2d, xout, None)

    p.finalize(block, sems, dbg=True)
    ctx.close()
    return nc



def _consts(core):
    eye = np.eye(128, dtype=np.float32)
    mask = np.zeros((4, 128, 512), dtype=np.float32)
    for j in range(4):
        for kp in range(128):
            q0 = 128 * j + kp
            if q0 > 0:
                mask[j, kp, 0:min(q0, 512)] = -30000.0
    bik = np.zeros((NTT * NKC, 128), dtype=np.float32)
    for qt in range(NTT):
        for kc in range(NKC):
            kloc = 512 * qt - WIN + 128 * kc + np.arange(128)
            v = MSLOPE * kloc + CSHIFT
            if core == 0:
                v = np.where(kloc < 0, -30000.0, v)
            bik[NKC * qt + kc] = v
    aqv = np.zeros((NTT, 512), dtype=np.float32)
    for qt in range(NTT):
        aqv[qt] = -MSLOPE * (512 * qt + np.arange(512))
    inda = np.zeros((1, 128), dtype=np.float32)
    inda[0, 0:64] = 1.0
    indb = np.zeros((1, 128), dtype=np.float32)
    indb[0, 64:128] = 1.0
    sel8 = np.zeros((8, 8 * 128), dtype=np.float32)
    for e_ in range(8):
        sel8[e_, 128 * e_:128 * e_ + 128] = 1.0
    gidx = (128 * ((core - 1) % 8) + np.arange(128)).astype(np.int32)
    return dict(eyec=eye, eyerc=eye, eye2c=(2.0 * eye).astype(np.float32),
                maskc=mask, biaskc=np.ascontiguousarray(bik.T), aqc=aqv.reshape(1, NTT, 512),
                indac=inda, indbc=indb, sel8c=sel8,
                gidxc=gidx.reshape(128, 1))


_taps = ()


def kernel(L0_attn_hidden_states, L0_attn_W_attn, L0_attn_b_attn,
           L0_attn_W_proj, L0_attn_b_proj, L0_ffn_gate_w, L0_ffn_w1,
           L0_ffn_w3, L0_ffn_w2, L0_ffn_a1, L0_ffn_b1, L0_ffn_a3, L0_ffn_b3,
           L0_ffn_a2, L0_ffn_b2):
    x = np.asarray(L0_attn_hidden_states, dtype=np.float32).reshape(S, D)
    shared = {
        "Wat": np.ascontiguousarray(np.asarray(L0_attn_W_attn, np.float32)),
        "bat": np.ascontiguousarray(
            np.asarray(L0_attn_b_attn, np.float32).reshape(48, 128)),
        "batT": np.ascontiguousarray(
            np.asarray(L0_attn_b_attn, np.float32).reshape(48, 128).T),
        "bprT": np.ascontiguousarray(
            np.asarray(L0_attn_b_proj, np.float32).reshape(16, 128).T),
        "Wpr": np.ascontiguousarray(np.asarray(L0_attn_W_proj, np.float32)),
        "bpr": np.ascontiguousarray(
            np.asarray(L0_attn_b_proj, np.float32).reshape(16, 128)),
        "gw": np.ascontiguousarray(np.asarray(L0_ffn_gate_w, np.float32)),
        "w1": np.ascontiguousarray(np.asarray(L0_ffn_w1, np.float32)),
        "w3": np.ascontiguousarray(np.asarray(L0_ffn_w3, np.float32)),
        "w2": np.ascontiguousarray(np.asarray(L0_ffn_w2, np.float32)),
        "a1": np.ascontiguousarray(np.asarray(L0_ffn_a1, np.float32)),
        "b1": np.ascontiguousarray(np.asarray(L0_ffn_b1, np.float32)),
        "a3": np.ascontiguousarray(np.asarray(L0_ffn_a3, np.float32)),
        "b3": np.ascontiguousarray(np.asarray(L0_ffn_b3, np.float32)),
        "a2": np.ascontiguousarray(np.asarray(L0_ffn_a2, np.float32)),
        "b2": np.ascontiguousarray(np.asarray(L0_ffn_b2, np.float32)),
    }
    in_maps = []
    for c in range(NCORES):
        m = dict(shared)
        m["xT"] = np.ascontiguousarray(x[TPC * c:TPC * (c + 1), :].T)
        m.update(_consts(c))
        in_maps.append(m)
    nc = bass.Bass("TRN2", target_bir_lowering=False)
    build_program(nc, taps=_taps)
    res = run_bass_kernel_spmd(nc, in_maps, core_ids=list(range(NCORES)))
    out = np.concatenate([r["xo"].T for r in res.results], axis=0)
    return out.reshape(1, S, D).astype(np.float32)
